# version 12
# baseline (speedup 1.0000x reference)
import numpy as np
import ml_dtypes
from contextlib import ExitStack

import concourse.bass as bass
import concourse.mybir as mybir
from concourse.bass_utils import run_bass_kernel_spmd

F32 = mybir.dt.float32
BF16 = mybir.dt.bfloat16
AX = mybir.AxisListType.X
AF = mybir.ActivationFunctionType
ALU = mybir.AluOpType

D = 2048
HID = 8192
WIN = 128
TPC = 2048
TT = 512
NT = TPC // TT
DC = D // 128
NG = 4
GD = D // NG
HS = HID // 128
RT = (TT + 128) // 128
UN = 8

BF = ml_dtypes.bfloat16


class Sched:

    def __init__(self):
        self.ops = {e: [] for e in "TVSGY"}
        self.cnt = {}
        for e in "TVS":
            self.cnt[e] = 0

    def op(self, e, fn, inc=0):
        assert e in "TVS"
        self.ops[e].append(("op", fn, inc, e))
        self.cnt[e] += inc
        return self.cnt[e]

    def dma(self, e, stream, fn):
        assert e in "GY"
        self.cnt.setdefault(stream, 0)
        self.ops[e].append(("op", fn, 16, stream))
        self.cnt[stream] += 16
        return self.cnt[stream]

    def wait(self, e, on, val):
        if val and val > 0:
            assert val <= self.cnt[on], (e, on, val, self.cnt[on])
            self.ops[e].append(("wait", on, val))

    def raw_wait(self, e, on, val):
        self.ops[e].append(("wait", on, val))

    def vpad_inc(self):
        self.op("V", lambda v: v.engine_nop())
        return self.op("V", lambda v: v.engine_nop(), inc=1)

    def emit(self, eng, e, sems):
        for entry in self.ops[e]:
            if entry[0] == "wait":
                eng.wait_ge(sems[entry[1]], entry[2])
            else:
                _, fn, inc, semname = entry
                ins = fn(eng)
                if inc:
                    ins.then_inc(sems[semname], inc)


def build_nc(debug=False):
    nc = bass.Bass("TRN2", target_bir_lowering=False)

    x_tok = nc.dram_tensor("x_tok", [TPC + 128, D], F32, kind="ExternalInput").ap()
    x_fm = nc.dram_tensor("x_fm", [D, TPC], F32, kind="ExternalInput").ap()
    wq = nc.dram_tensor("wq", [NG, 128, DC, GD], F32, kind="ExternalInput").ap()
    wk = nc.dram_tensor("wk", [NG, 128, DC, GD], F32, kind="ExternalInput").ap()
    wv = nc.dram_tensor("wv", [NG, 128, DC, GD], F32, kind="ExternalInput").ap()
    wo = nc.dram_tensor("wo", [128, DC, DC, 128], F32, kind="ExternalInput").ap()
    wg = nc.dram_tensor("wg", [128, HS, DC, 128], F32, kind="ExternalInput").ap()
    wu = nc.dram_tensor("wu", [128, HS, DC, 128], F32, kind="ExternalInput").ap()
    wd = nc.dram_tensor("wd", [128, DC, HS, 128], F32, kind="ExternalInput").ap()
    mask_std = nc.dram_tensor("mask_std", [128, 256], BF16, kind="ExternalInput").ap()
    mask_first = nc.dram_tensor("mask_first", [128, 256], BF16, kind="ExternalInput").ap()
    ident_in = nc.dram_tensor("ident", [128, 128], BF16, kind="ExternalInput").ap()
    o_fm = nc.dram_tensor("o_fm", [D, TPC], F32, kind="ExternalOutput").ap()
    dbg = {}
    if debug:
        dbg["xnfm"] = nc.dram_tensor("d_xnfm", [128, DC, TT + 128], BF16, kind="ExternalOutput").ap()
        dbg["qg"] = nc.dram_tensor("d_qg", [128, 4, TT], BF16, kind="ExternalOutput").ap()
        dbg["kg"] = nc.dram_tensor("d_kg", [128, 4, TT + 128], BF16, kind="ExternalOutput").ap()
        dbg["vtm"] = nc.dram_tensor("d_vtm", [128, RT, GD], BF16, kind="ExternalOutput").ap()
        dbg["pb"] = nc.dram_tensor("d_pb", [128, 4, 256], BF16, kind="ExternalOutput").ap()
        dbg["pm"] = nc.dram_tensor("d_pm", [128, 4, 256], BF16, kind="ExternalOutput").ap()
        dbg["heads"] = nc.dram_tensor("d_heads", [128, DC, TT], BF16, kind="ExternalOutput").ap()
        dbg["hbuf"] = nc.dram_tensor("d_hbuf", [128, HS, TT], BF16, kind="ExternalOutput").ap()
        dbg["sums"] = nc.dram_tensor("d_sums", [128, 2, 4], F32, kind="ExternalOutput").ap()

    es = ExitStack()
    sb = lambda n, sh, dt: es.enter_context(nc.sbuf_tensor(n, sh, dt))
    psum = lambda n, sh, dt: es.enter_context(nc.psum_tensor(n, sh, dt))

    xrow = [sb(f"xrow{i}", [128, D], F32) for i in range(2)]
    xn_tok = [sb(f"xntok{i}", [128, D], BF16) for i in range(2)]
    sq_scr = sb("sq_scr", [128, 512], F32)
    st_s1 = sb("st_s1", [128, 5], F32)
    st_s2q = sb("st_s2q", [128, 5, 4], F32)
    st_mean = sb("st_mean", [128, 5], F32)
    st_msq = sb("st_msq", [128, 5], F32)
    st_msq2 = sb("st_msq2", [128, 5], F32)
    st_var = sb("st_var", [128, 5], F32)
    st_rstd = sb("st_rstd", [128, 5], F32)
    st_nbias = sb("st_nbias", [128, 5], F32)
    xn_fm = sb("xn_fm", [128, DC, TT + 128], BF16)
    warena = sb("warena", [128, 20480], BF16)
    qz = [sb(f"qz{i}", [128, 4, TT], BF16) for i in range(2)]
    k_g = sb("k_g", [128, 4, TT + 128], BF16)
    v_tm = sb("v_tm", [128, RT, GD], BF16)
    heads = sb("heads", [128, DC, TT], BF16)
    hbuf = sb("hbuf", [128, HS, TT], BF16)
    silu_t = [sb(f"silu{i}", [128, TT], BF16) for i in range(2)]
    cbias = sb("cbias", [128, 1], F32)
    sumsb = sb("sumsb", [128, 2, 4], F32)
    recipb = sb("recipb", [128, 2, 4], F32)
    pmbuf = [sb(f"pmbuf{i}", [128, 4, 256], BF16) for i in range(2)]
    ptbuf = [sb(f"ptbuf{i}", [128, 1024], BF16) for i in range(2)]
    xfmb = [sb(f"xfmb{i}", [128, TT], F32) for i in range(2)]
    k_halo = [sb(f"khalo{i}", [128, 4, 128], BF16) for i in range(NG)]
    v_halo = [sb(f"vhalo{i}", [128, GD], BF16) for i in range(NG)]
    mskS = sb("mskS", [128, 256], BF16)
    mskF = sb("mskF", [128, 256], BF16)
    ident = sb("identb", [128, 128], BF16)

    def arena(off, shape):
        n = int(np.prod(shape))
        apv = warena[:, off:off + n]
        return apv.rearrange("p (a b) -> p a b", b=shape[1])

    qkv_slot = [arena(0, [DC, GD]), arena(8192, [DC, GD])]
    wg_slot = [arena(0, [DC, 128]), arena(2048, [DC, 128]), arena(4096, [DC, 128])]
    wu_slot = [arena(6144, [DC, 128]), arena(8192, [DC, 128]), arena(10240, [DC, 128])]
    wo_slot = [arena(0, [DC, 128]), arena(2048, [DC, 128])]
    wd_slot = [arena(4096, [HS, 128]), arena(12288, [HS, 128])]

    ps_ab = [psum("ps_a", [128, 512], F32), psum("ps_b", [128, 512], F32)]
    ps_s = [psum("ps_s0", [128, 1024], F32), psum("ps_s1", [128, 1024], F32)]
    ps_pt = [psum("ps_pt0", [128, 1024], BF16), psum("ps_pt1", [128, 1024], BF16)]

    sems = {}
    for e in ["T", "V", "S", "c", "xr0", "xr1", "xf0", "xf1", "st0", "st1",
              "qk0", "qk1", "wgu0", "wgu1", "wgu2", "wo0", "wo1", "wd0", "wd1"]:
        sems[e] = es.enter_context(nc.semaphore(f"sem_{e}"))

    S = Sched()
    ev = {}
    guard = {}

    def rec(key, eng, cnt):
        d = guard.setdefault(key, {})
        d[eng] = max(d.get(eng, 0), cnt)

    def gwait(waiter, key):
        for eng, cnt in guard.get(key, {}).items():
            if eng != waiter:
                S.wait(waiter, eng, cnt)

    S.dma("Y", "c", lambda y: y.dma_start(out=mskS[:], in_=mask_std))
    S.dma("Y", "c", lambda y: y.dma_start(out=mskF[:], in_=mask_first))
    consts = S.dma("Y", "c", lambda y: y.dma_start(out=ident[:], in_=ident_in))
    S.wait("T", "c", consts)
    S.wait("V", "c", consts)

    S.op("V", lambda v: v.memset(qz[0][:], 0.0))
    S.op("V", lambda v: v.memset(qz[1][:], 0.0))
    S.op("V", lambda v: v.memset(cbias[:], -4.0))
    qz_zeroed = S.vpad_inc()
    S.wait("S", "V", qz_zeroed)
    S.wait("T", "V", qz_zeroed)

    arena_T_free = 0

    ln_seq = [0]

    def emit_ln_stats(tl, rt):
        grt = tl * RT + rt
        seq = ln_seq[0]
        ln_seq[0] += 1
        p2 = seq % 2
        sl = seq % 5
        ev[("ln_idx", grt)] = (seq, p2, sl)
        row0 = tl * TT + 128 * rt
        pe = ev.get(("ln_apply_seq", seq - 5))
        if pe is not None:
            S.wait("V", "S", pe)
        gwait("Y", ("xrow", p2))
        ld = S.dma("Y", f"xr{p2}", lambda y, r0=row0, p=p2: y.dma_start(
            out=xrow[p][:], in_=x_tok[r0:r0 + 128, :]))
        S.wait("V", f"xr{p2}", ld)
        vsum = S.op("V", lambda v, p=p2, q=sl: v.reduce_sum(
            st_s1[:, q:q + 1], xrow[p][:], axis=AX), inc=1)
        rec(("xrow", p2), "V", vsum)
        S.wait("S", f"xr{p2}", ld)
        pe = ev.get(("sq_done", grt - 1))
        if pe is not None:
            S.wait("S", "S", pe)
        for qq in range(4):
            sq_done = S.op("S", lambda s, p=p2, w=sl, q=qq: s.activation(
                sq_scr[:], xrow[p][:, 512 * q:512 * (q + 1)], AF.Square,
                accum_out=st_s2q[:, w, q:q + 1]), inc=1)
            if qq < 3:
                S.wait("S", "S", sq_done)
        ev[("sq_done", grt)] = sq_done
        S.wait("V", "S", sq_done)
        c = S.op("V", lambda v, q=sl: v.reduce_sum(
            st_msq[:, q:q + 1], st_s2q[:, q, :], axis=AX), inc=1)
        S.wait("V", "V", c)
        S.op("V", lambda v: v.engine_nop())
        c = S.op("V", lambda v, q=sl: v.tensor_scalar_mul(
            st_mean[:, q:q + 1], st_s1[:, q:q + 1], 1.0 / D), inc=1)
        c = S.op("V", lambda v, q=sl: v.tensor_scalar_mul(
            st_msq[:, q:q + 1], st_msq[:, q:q + 1], 1.0 / D), inc=1)
        S.wait("V", "V", c)
        S.op("V", lambda v: v.engine_nop())
        c = S.op("V", lambda v, q=sl: v.tensor_scalar(
            st_msq2[:, q:q + 1], st_mean[:, q:q + 1], st_mean[:, q:q + 1],
            None, ALU.mult), inc=1)
        S.wait("V", "V", c)
        S.op("V", lambda v: v.engine_nop())
        S.op("V", lambda v: v.engine_nop())
        S.op("V", lambda v, q=sl: v.tensor_scalar(
            st_var[:, q:q + 1], st_msq[:, q:q + 1], st_msq2[:, q:q + 1], 1e-5,
            ALU.subtract, ALU.add))
        var_done = S.vpad_inc()
        S.wait("S", "V", var_done)
        sqrt_done = S.op("S", lambda s, q=sl: s.activation(
            st_rstd[:, q:q + 1], st_var[:, q:q + 1], AF.Sqrt), inc=1)
        S.wait("V", "S", sqrt_done)
        c = S.op("V", lambda v, q=sl: v.reciprocal(
            st_rstd[:, q:q + 1], st_rstd[:, q:q + 1]), inc=1)
        S.wait("V", "V", c)
        S.op("V", lambda v: v.engine_nop())
        S.op("V", lambda v: v.engine_nop())
        S.op("V", lambda v, q=sl: v.tensor_scalar(
            st_nbias[:, q:q + 1], st_mean[:, q:q + 1], st_rstd[:, q:q + 1], -1.0,
            ALU.mult, ALU.mult))
        ev[("nbias", grt)] = S.vpad_inc()

    def emit_ln_app(tl, rt):
        grt = tl * RT + rt
        seq, p2, sl = ev[("ln_idx", grt)]
        S.wait("S", "V", ev[("nbias", grt)])
        gwait("S", ("xn_tok", p2))
        apply_done = S.op("S", lambda s, p=p2, q=sl: s.activation(
            xn_tok[p][:], xrow[p][:], AF.Identity,
            bias=st_nbias[:, q:q + 1], scale=st_rstd[:, q:q + 1]), inc=1)
        rec(("xrow", p2), "S", apply_done)
        ev[("ln_apply", grt)] = apply_done
        ev[("ln_apply_seq", seq)] = apply_done

    def emit_ln_tr(tl, rt):
        grt = tl * RT + rt
        seq, p2, sl = ev[("ln_idx", grt)]
        apply_done = ev[("ln_apply", grt)]
        S.wait("T", "S", apply_done)
        for half in range(2):
            gwait("T", ("pt", half))
            for kk in range(8):
                dinc = half * 8 + kk
                last = S.op("T", lambda t, p=p2, d=dinc, h=half, k=kk: t.transpose(
                    ps_pt[h][:, 128 * k:128 * (k + 1)],
                    xn_tok[p][:, 128 * d:128 * (d + 1)], ident[:]),
                    inc=1 if kk == 7 else 0)
            rec(("xn_tok", p2), "T", last)
            S.wait("S", "T", last)
            tev = S.op("S", lambda s, h=half, r=rt: s.copy(
                xn_fm[:, 8 * h:8 * (h + 1), 128 * r:128 * (r + 1)],
                ps_pt[h].ap().rearrange("p (a b) -> p a b", b=128)), inc=1)
            rec(("pt", half), "S", tev)

    for ti in range(NT):
        if ti == 0:
            emit_ln_stats(0, 0)
            emit_ln_stats(0, 1)
            for rt in range(RT):
                emit_ln_app(0, rt)
                if rt + 2 < RT:
                    emit_ln_stats(0, rt + 2)
                emit_ln_tr(0, rt)
            ev[("ln_done", 0)] = S.cnt["S"]
        ln_done_s = ev[("ln_done", ti)]
        if debug and ti == 0:
            S.wait("Y", "S", ln_done_s)
            S.dma("Y", "c", lambda y: y.dma_start(out=dbg["xnfm"], in_=xn_fm[:]))
            for e in "TVS":
                S.wait(e, "c", S.cnt["c"])

        slab_idx = 0

        def load_slab(src, si):
            pe = ev.get(("slab_free", ti, si - 2))
            if pe is None and si < 2:
                S.wait("G", "T", arena_T_free)
            elif pe is not None:
                S.wait("G", "T", pe)
            return S.dma("G", f"qk{si % 2}", lambda gq, s=src, slot=si % 2: gq.dma_start(
                out=qkv_slot[slot][:], in_=s))

        def proj_mms(sl, which, g, carry):
            if which == "q":
                for ds in range(4):
                    par = ds % 2
                    gwait("T", ("ab", par))
                    for dinc in range(DC):
                        last = S.op("T", lambda t, s=sl, d=ds, k=dinc, p=par: t.matmul(
                            ps_ab[p][:], qkv_slot[s][:, k, 128 * d:128 * (d + 1)],
                            xn_fm[:, k, 128:128 + TT],
                            start=(k == 0), stop=(k == DC - 1)),
                            inc=1 if dinc == DC - 1 else 0)
                    S.wait("S", "T", last)
                    S.op("S", lambda s, d=ds, p=par: s.copy(
                        qz[0][0:64, d, :], ps_ab[p][0:64, :]))
                    e = S.op("S", lambda s, d=ds, p=par: s.copy(
                        qz[1][64:128, d, :], ps_ab[p][64:128, :]), inc=1)
                    rec(("ab", par), "S", e)
            elif which == "k":
                if carry:
                    S.wait("S", "T", ev[("attnT_g", ti - 1 if g == 0 else ti,
                                         (g - 1) % NG)])
                    S.op("S", lambda s, gg=g: s.copy(
                        k_g[:, :, 0:128], k_halo[gg][:]))
                for ds in range(4):
                    par = ds % 2
                    gwait("T", ("ab", par))
                    lo = 128 if carry else 0
                    for dinc in range(DC):
                        S.op("T", lambda t, s=sl, d=ds, k=dinc, p=par, l=lo: t.matmul(
                            ps_ab[p][:], qkv_slot[s][:, k, 128 * d:128 * (d + 1)],
                            xn_fm[:, k, l:l + 512],
                            start=(k == 0), stop=(k == DC - 1)))
                    if not carry:
                        gwait("T", ("ab", 1 - par))
                        for dinc in range(DC):
                            last = S.op("T", lambda t, s=sl, d=ds, k=dinc, p=par: t.matmul(
                                ps_ab[1 - p][:, 0:128],
                                qkv_slot[s][:, k, 128 * d:128 * (d + 1)],
                                xn_fm[:, k, 512:640],
                                start=(k == 0), stop=(k == DC - 1)),
                                inc=1 if dinc == DC - 1 else 0)
                    else:
                        S.ops["T"][-1] = ("op", S.ops["T"][-1][1], 1, "T")
                        S.cnt["T"] += 1
                        last = S.cnt["T"]
                    S.wait("S", "T", last)
                    e1 = S.op("S", lambda s, d=ds, p=par, l=(128 if carry else 0):
                              s.copy(k_g[:, d, l:l + 512], ps_ab[p][:]), inc=1)
                    rec(("ab", par), "S", e1)
                    if not carry:
                        e2 = S.op("S", lambda s, d=ds, p=par: s.copy(
                            k_g[:, d, 512:640], ps_ab[1 - p][:, 0:128]), inc=1)
                        rec(("ab", 1 - par), "S", e2)
                S.op("S", lambda s, gg=g: s.copy(
                    k_halo[gg][:], k_g[:, :, 512:640]))
            else:
                if carry:
                    S.op("S", lambda s, gg=g: s.copy(v_tm[:, 0, :], v_halo[gg][:]))
                for tc in (range(1, RT) if carry else range(RT)):
                    par = tc % 2
                    gwait("T", ("ab", par))
                    for dinc in range(DC):
                        last = S.op("T", lambda t, s=sl, c=tc, k=dinc, p=par: t.matmul(
                            ps_ab[p][:], xn_fm[:, k, 128 * c:128 * (c + 1)],
                            qkv_slot[s][:, k, :],
                            start=(k == 0), stop=(k == DC - 1)),
                            inc=1 if dinc == DC - 1 else 0)
                    S.wait("S", "T", last)
                    e = S.op("S", lambda s, c=tc, p=par: s.copy(
                        v_tm[:, c, :], ps_ab[p][:]), inc=1)
                    rec(("ab", par), "S", e)
                S.op("S", lambda s, gg=g: s.copy(v_halo[gg][:], v_tm[:, 4, :]))

        for g in range(NG):
            ld_q = load_slab(wq[g], slab_idx); sl_q = slab_idx % 2; slab_idx += 1
            ld_k = load_slab(wk[g], slab_idx); sl_k = slab_idx % 2; slab_idx += 1

            S.wait("T", f"qk{sl_q}", ld_q)
            if g == 0:
                S.wait("T", "S", ln_done_s)
            proj_mms(sl_q, "q", g, ti > 0)
            ev[("slab_free", ti, slab_idx - 2)] = S.cnt["T"]
            ld_v = load_slab(wv[g], slab_idx); sl_v = slab_idx % 2; slab_idx += 1
            S.wait("T", f"qk{sl_k}", ld_k)
            proj_mms(sl_k, "k", g, ti > 0)
            ev[("slab_free", ti, slab_idx - 2)] = S.cnt["T"]
            S.wait("T", f"qk{sl_v}", ld_v)
            proj_mms(sl_v, "v", g, ti > 0)
            ev[("slab_free", ti, slab_idx - 1)] = S.cnt["T"]
            if debug and ti == 0 and g == 0:
                S.wait("Y", "S", S.cnt["S"])
                S.dma("Y", "c", lambda y: y.dma_start(out=dbg["qg"], in_=qz[0][:]))
                S.dma("Y", "c", lambda y: y.dma_start(out=dbg["kg"], in_=k_g[:]))
                S.dma("Y", "c", lambda y: y.dma_start(out=dbg["vtm"], in_=v_tm[:]))
                for e in "TVS":
                    S.wait(e, "c", S.cnt["c"])

            gu0 = (ti * NG + g) * UN
            qkv_evacs_done = S.cnt["S"]

            def unit_qb(u, j):
                return 2 * (u % 2) + (j % 2)

            def t1_scores(u):
                gu = gu0 + u
                par = gu % 2
                if u == 0:
                    S.wait("T", "S", qkv_evacs_done)
                gwait("T", ("ss", par))
                hp = u // 2
                for j in range(4):
                    h_in = j // 2
                    qb = unit_qb(u, j)
                    last = S.op("T", lambda t, p=par, hh=hp, hi=h_in, q=qb, jj=j: t.matmul(
                        ps_s[p][:, 256 * jj:256 * (jj + 1)],
                        qz[hi][:, hh, 128 * q:128 * (q + 1)],
                        k_g[:, hh, 128 * q:128 * q + 256],
                        start=True, stop=True),
                        inc=1 if j == 3 else 0)
                ev[("scores", gu)] = last

            def v3_stt(u):
                gu = gu0 + u
                par = gu % 2
                S.wait("V", "S", ev[("exp", gu)])
                pe = ev.get(("pt", gu - 2))
                if pe is not None:
                    S.wait("V", "T", pe)
                for j in range(4):
                    qb = unit_qb(u, j)
                    mk = mskF if (ti == 0 and qb == 0) else mskS
                    c = S.op("V", lambda v, p=par, jj=j, m=mk: v.scalar_tensor_tensor(
                        pmbuf[p][:, jj, :], pmbuf[p][:, jj, :], 1.0, m[:],
                        ALU.mult, ALU.mult,
                        accum_out=sumsb[:, p, jj:jj + 1]), inc=1 if j == 3 else 0)
                ev[("stt", gu)] = c

            def v_mid(u):
                if not (0 <= u - 1 < UN):
                    return
                gp = gu0 + u - 1
                pp = gp % 2
                S.wait("V", "V", ev[("stt", gp)])
                S.op("V", lambda v: v.engine_nop())
                S.op("V", lambda v: v.engine_nop())
                S.op("V", lambda v: v.engine_nop())
                c = S.op("V", lambda v, p=pp: v.reciprocal(
                    recipb[:, p, :], sumsb[:, p, :]), inc=1)
                S.wait("V", "V", c)
                S.op("V", lambda v: v.engine_nop())
                S.op("V", lambda v: v.engine_nop())
                S.op("V", lambda v: v.engine_nop())
                pe = ev.get(("pt", gp - 2))
                if pe is not None:
                    S.wait("V", "T", pe)
                for j in range(4):
                    S.op("V", lambda v, p=pp, jj=j: v.tensor_scalar_mul(
                        pmbuf[p][:, jj, :], pmbuf[p][:, jj, :],
                        recipb[:, p, jj:jj + 1]))
                ev[("pscale", gp)] = S.vpad_inc()

            def v_mid(u):
                if not (0 <= u - 1 < UN):
                    return
                gp = gu0 + u - 1
                pp = gp % 2
                S.wait("V", "V", ev[("stt", gp)])
                S.op("V", lambda v: v.engine_nop())
                S.op("V", lambda v: v.engine_nop())
                S.op("V", lambda v: v.engine_nop())
                c = S.op("V", lambda v, p=pp: v.reciprocal(
                    recipb[:, p, :], sumsb[:, p, :]), inc=1)
                S.wait("V", "V", c)
                S.op("V", lambda v: v.engine_nop())
                S.op("V", lambda v: v.engine_nop())
                S.op("V", lambda v: v.engine_nop())
                pe = ev.get(("pt", gp - 2))
                if pe is not None:
                    S.wait("V", "T", pe)
                for j in range(4):
                    S.op("V", lambda v, p=pp, jj=j: v.tensor_scalar_mul(
                        pmbuf[p][:, jj, :], pmbuf[p][:, jj, :],
                        recipb[:, p, jj:jj + 1]))
                ev[("pscale", gp)] = S.vpad_inc()

            def s1_exp(u):
                gu = gu0 + u
                par = gu % 2
                S.wait("S", "T", ev[("scores", gu)])
                pe = ev.get(("pt", gu - 2))
                if pe is not None:
                    S.wait("S", "T", pe)
                for half in range(2):
                    last = S.op("S", lambda s, p=par, h=half: s.activation(
                        pmbuf[p][:, 2 * h:2 * h + 2, :],
                        ps_s[p][:, 512 * h:512 * (h + 1)].rearrange(
                            "p (a b) -> p a b", b=256),
                        AF.Exp, bias=cbias[:], scale=0.125),
                        inc=1 if half == 1 else 0)
                ev[("exp", gu)] = last
                rec(("ss", par), "S", last)

            def t2_pt(u):
                gu = gu0 + u
                par = gu % 2
                S.wait("T", "V", ev[("pscale", gu)])
                gwait("T", ("pt", par))
                for j in range(4):
                    for cc in range(2):
                        idx = 2 * j + cc
                        last = S.op("T", lambda t, p=par, jj=j, c=cc, i=idx: t.transpose(
                            ps_pt[p][:, 128 * i:128 * (i + 1)],
                            pmbuf[p][:, jj, 128 * c:128 * (c + 1)], ident[:]),
                            inc=1 if idx == 7 else 0)
                ev[("pt", gu)] = last

            def s2_ptevac(u):
                gu = gu0 + u
                par = gu % 2
                S.wait("S", "T", ev[("pt", gu)])
                pe = ev.get(("pv", gu - 2))
                if pe is not None:
                    S.wait("S", "T", pe)
                e = S.op("S", lambda s, p=par: s.copy(ptbuf[p][:], ps_pt[p].ap()), inc=1)
                ev[("ptevac", gu)] = e
                rec(("pt", par), "S", e)

            def t3_pv(u):
                gu = gu0 + u
                par = gu % 2
                hp = u // 2
                S.wait("T", "S", ev[("ptevac", gu)])
                gwait("T", ("ab", par))
                first = True
                for jq in range(2):
                    for cc in range(2):
                        for h_in in range(2):
                            j = 2 * h_in + jq
                            qb = unit_qb(u, j)
                            idx = 2 * j + cc
                            reg = 2 * jq + h_in
                            is_last = (jq, cc, h_in) == (1, 1, 1)
                            last = S.op("T", lambda t, p=par, c=cc, q=qb, hh=hp,
                                        i=idx, r=reg, st=first, sp=is_last: t.matmul(
                                ps_ab[p][:, 128 * r:128 * (r + 1)],
                                v_tm[:, q + c, 128 * hh:128 * (hh + 1)],
                                ptbuf[p][:, 128 * i:128 * (i + 1)],
                                start=st, stop=sp, skip_group_check=True),
                                inc=1 if is_last else 0)
                            first = False
                ev[("pv", gu)] = last

            def s3_headsevac(u):
                gu = gu0 + u
                par = gu % 2
                hp = u // 2
                qh = u % 2
                S.wait("S", "T", ev[("pv", gu)])
                for h_in in range(2):
                    e = S.op("S", lambda s, p=par, hh=hp, q=qh, gg=g, hi=h_in: s.copy(
                        heads[64 * hi:64 * hi + 64, 4 * gg + hh,
                              256 * q:256 * q + 256].rearrange(
                                  "p (a b) -> p a b", b=128),
                        ps_ab[p][64 * hi:64 * hi + 64, :].rearrange(
                            "p (a b c) -> p a b c", a=2, c=128)[:, :, hi, :]),
                        inc=1 if h_in == 1 else 0)
                rec(("ab", par), "S", e)

            for u in range(UN + 2):
                if debug and ti == 0 and g == 0 and u == 1:
                    S.wait("Y", "S", ev[("exp", gu0)])
                    S.dma("Y", "c", lambda y: y.dma_start(out=dbg["pb"], in_=pmbuf[0][:]))
                    for e in "TVS":
                        S.wait(e, "c", S.cnt["c"])
                if debug and ti == 0 and g == 0 and u == 2:
                    S.wait("Y", "V", ev[("pscale", gu0)])
                    S.dma("Y", "c", lambda y: y.dma_start(out=dbg["pm"], in_=pmbuf[0][:]))
                    S.dma("Y", "c", lambda y: y.dma_start(out=dbg["sums"], in_=sumsb[:]))
                    for e in "TVS":
                        S.wait(e, "c", S.cnt["c"])
                if u < UN:
                    t1_scores(u)
                if 0 <= u - 1 < UN:
                    v3_stt(u - 1)
                v_mid(u)
                if u < UN:
                    s1_exp(u)
                if 0 <= u - 1 < UN:
                    t2_pt(u - 1)
                    s2_ptevac(u - 1)
                if 0 <= u - 2 < UN:
                    t3_pv(u - 2)
                    s3_headsevac(u - 2)
            ev[("attnT_g", ti, g)] = S.cnt["T"]

        attn_T_done = S.cnt["T"]
        heads_done_s = S.cnt["S"]
        ev[("attnT_g", ti, NG - 1)] = attn_T_done

        for hs in range(HS):
            par = hs % 2
            wsl = hs % 3
            if ti + 1 < NT:
                if hs == 36:
                    emit_ln_stats(ti + 1, 1)
                elif hs == 40:
                    emit_ln_app(ti + 1, 1)
                elif hs == 44:
                    emit_ln_stats(ti + 1, 2)
                elif hs == 48:
                    emit_ln_app(ti + 1, 2)
                elif hs == 52:
                    emit_ln_stats(ti + 1, 3)
                elif hs == 56:
                    emit_ln_stats(ti + 1, 4)
            pe = ev.get(("wgu_free", ti, hs - 3))
            if pe is None and hs < 3:
                S.wait("G", "T", ev[("slab_free", ti, 11)])
            elif pe is not None:
                S.wait("G", "T", pe)
            ld_g = S.dma("G", f"wgu{wsl}", lambda gq, h=hs, p=wsl: gq.dma_start(
                out=wg_slot[p][:], in_=wg[:, h]))
            ld_u = S.dma("G", f"wgu{wsl}", lambda gq, h=hs, p=wsl: gq.dma_start(
                out=wu_slot[p][:], in_=wu[:, h]))
            S.wait("T", f"wgu{wsl}", ld_u)
            gwait("T", ("ss", par))
            for dinc in range(DC):
                S.op("T", lambda t, k=dinc, p=par, w=wsl: t.matmul(
                    ps_s[p][:, 0:512], wg_slot[w][:, k, :], xn_fm[:, k, 128:128 + TT],
                    start=(k == 0), stop=(k == DC - 1)))
            for dinc in range(DC):
                last = S.op("T", lambda t, k=dinc, p=par, w=wsl: t.matmul(
                    ps_s[p][:, 512:1024], wu_slot[w][:, k, :], xn_fm[:, k, 128:128 + TT],
                    start=(k == 0), stop=(k == DC - 1)),
                    inc=1 if dinc == DC - 1 else 0)
            ev[("wgu_free", ti, hs)] = last
            S.wait("S", "T", last)
            sil = S.op("S", lambda s, p=par: s.activation(
                silu_t[p][:], ps_s[p][:, 0:512], AF.Silu), inc=1)
            rec(("ss", par), "S", sil)
            S.wait("V", "S", sil)
            hm = S.op("V", lambda v, h=hs, p=par: v.tensor_mul(
                hbuf[:, h, :], silu_t[p][:], ps_s[p][:, 512:1024]), inc=1)
            rec(("ss", par), "V", hm)
        ffn_h_done_v = S.vpad_inc()
        gate_T_done = S.cnt["T"]
        if debug and ti == 0:
            S.wait("Y", "S", heads_done_s)
            S.dma("Y", "c", lambda y: y.dma_start(out=dbg["heads"], in_=heads[:]))
            S.wait("Y", "V", ffn_h_done_v)
            S.dma("Y", "c", lambda y: y.dma_start(out=dbg["hbuf"], in_=hbuf[:]))
            for e in "TVS":
                S.wait(e, "c", S.cnt["c"])

        for ds in range(DC):
            par = ds % 2
            osl = (ds + 1) % 2
            pe = ev.get(("wod_free", ti, ds - 2))
            if pe is not None:
                S.wait("G", "T", pe)
            elif ds == 0:
                S.wait("G", "T", ev[("slab_free", ti, 11)])
            elif ds == 1:
                S.wait("G", "T", ev[("wgu_free", ti, HS - 1)])
            ld_d = S.dma("G", f"wd{osl}", lambda gq, d=ds, p=osl: gq.dma_start(
                out=wd_slot[p][:], in_=wd[:, d]))
            if ds < 2:
                S.wait("G", "T", ev[("wgu_free", ti, HS - 1 if ds == 0 else HS - 3)])
            ld_o = S.dma("G", f"wo{par}", lambda gq, d=ds, p=par: gq.dma_start(
                out=wo_slot[p][:], in_=wo[:, d]))
            pst_x = ev.get(("store", ti, ds - 2))
            if pst_x is None and ds < 2:
                pst_x = ev.get(("store", ti - 1, DC - 2 + ds))
            if pst_x is not None:
                S.wait("Y", f"st{par}", pst_x)
            ld_x = S.dma("Y", f"xf{par}", lambda y, d=ds, t=ti, p=par: y.dma_start(
                out=xfmb[p][:], in_=x_fm[128 * d:128 * (d + 1), TT * t:TT * (t + 1)]))
            S.wait("T", f"wd{osl}", ld_d)
            S.wait("T", f"wo{par}", ld_o)
            if ds == 0:
                S.wait("T", "V", ffn_h_done_v)
                S.wait("T", "S", heads_done_s)
            gwait("T", ("ab", par))
            for hdc in range(DC):
                S.op("T", lambda t, k=hdc, p=par: t.matmul(
                    ps_ab[p][:], wo_slot[p][:, k, :], heads[:, k, :],
                    start=(k == 0), stop=False))
            for hc in range(HS):
                last = S.op("T", lambda t, k=hc, p=par, o=osl: t.matmul(
                    ps_ab[p][:], wd_slot[o][:, k, :], hbuf[:, k, :],
                    start=False, stop=(k == HS - 1)),
                    inc=1 if hc == HS - 1 else 0)
            ev[("wod_free", ti, ds)] = last
            S.wait("V", "T", last)
            S.wait("V", f"xf{par}", ld_x)
            pst = ev.get(("store", ti, ds - 2))
            if pst is None and ds < 2:
                pst = ev.get(("store", ti - 1, DC - 2 + ds))
            if pst is not None:
                S.wait("V", f"st{par}", pst)
            S.op("V", lambda v, p=par: v.tensor_add(
                xfmb[p][:], ps_ab[p][:], xfmb[p][:]))
            oe = S.vpad_inc()
            ev[("outevac", ti, ds)] = oe
            rec(("ab", par), "V", oe)
            S.wait("Y", "V", oe)
            ev[("store", ti, ds)] = S.dma("Y", f"st{par}", lambda y, d=ds, t=ti, p=par: y.dma_start(
                out=o_fm[128 * d:128 * (d + 1), TT * t:TT * (t + 1)], in_=xfmb[p][:]))
        if ti + 1 < NT:
            emit_ln_tr(ti + 1, 1)
            emit_ln_app(ti + 1, 3)
            emit_ln_tr(ti + 1, 2)
            emit_ln_app(ti + 1, 4)
            emit_ln_tr(ti + 1, 3)
            emit_ln_tr(ti + 1, 4)
            ev[("ln_done", ti + 1)] = S.cnt["S"]
        arena_T_free = S.cnt["T"]

    for sname in ["c", "xr0", "xr1", "xf0", "xf1", "st0", "st1"]:
        if S.cnt.get(sname):
            S.raw_wait("Y", sname, S.cnt[sname])
    for sname in ["qk0", "qk1", "wgu0", "wgu1", "wgu2", "wo0", "wo1", "wd0", "wd1"]:
        if S.cnt.get(sname):
            S.raw_wait("G", sname, S.cnt[sname])

    with nc.Block() as block:
        @block.tensor
        def _(t):
            S.emit(t, "T", sems)

        @block.vector
        def _(v):
            S.emit(v, "V", sems)

        @block.scalar
        def _(s):
            S.emit(s, "S", sems)

        @block.gpsimd
        def _(gq):
            S.emit(gq, "G", sems)

        @block.sync
        def _(y):
            S.emit(y, "Y", sems)

    es.close()
    return nc


def _prep_weights(Wq, Wk, Wv, Wo, Wg, Wu, Wd):
    def qkv_layout(W):
        return np.ascontiguousarray(W.reshape(NG, GD, DC, 128).transpose(0, 3, 2, 1))

    return (qkv_layout(Wq), qkv_layout(Wk), qkv_layout(Wv),
            np.ascontiguousarray(Wo.reshape(DC, 128, DC, 128).transpose(3, 0, 2, 1)),
            np.ascontiguousarray(Wg.reshape(HS, 128, DC, 128).transpose(3, 0, 2, 1)),
            np.ascontiguousarray(Wu.reshape(HS, 128, DC, 128).transpose(3, 0, 2, 1)),
            np.ascontiguousarray(Wd.reshape(DC, 128, HS, 128).transpose(3, 0, 2, 1)))


def _masks():
    r = np.arange(128)[:, None]
    c = np.arange(256)[None, :]
    band = (r <= c) & (c <= r + WIN)
    m_std = band.astype(np.float32).astype(BF)
    m_first = (band & (c >= 128)).astype(np.float32).astype(BF)
    return m_std, m_first


def kernel(L0_attn_x, L0_attn_Wq, L0_attn_Wk, L0_attn_Wv, L0_attn_Wo,
           L0_ffn_w_gate, L0_ffn_w_up, L0_ffn_w_down):
    x = np.asarray(L0_attn_x)
    wq, wk, wv, wo, wg, wu, wd = _prep_weights(
        np.asarray(L0_attn_Wq), np.asarray(L0_attn_Wk), np.asarray(L0_attn_Wv),
        np.asarray(L0_attn_Wo), np.asarray(L0_ffn_w_gate), np.asarray(L0_ffn_w_up),
        np.asarray(L0_ffn_w_down))
    m_std, m_first = _masks()
    ident = np.eye(128, dtype=BF)

    nc = build_nc()
    in_maps = []
    for c in range(8):
        b = c // 4
        s = (c % 4) * TPC
        if s == 0:
            halo = np.zeros((128, D), dtype=np.float32)
            mf = m_first
        else:
            halo = x[b, s - 128:s, :]
            mf = m_std
        x_tok = np.ascontiguousarray(np.concatenate([halo, x[b, s:s + TPC, :]], axis=0))
        x_fm = np.ascontiguousarray(x[b, s:s + TPC, :].T)
        in_maps.append({
            "x_tok": x_tok, "x_fm": x_fm,
            "wq": wq, "wk": wk, "wv": wv, "wo": wo,
            "wg": wg, "wu": wu, "wd": wd,
            "mask_std": m_std, "mask_first": mf, "ident": ident,
        })

    res = run_bass_kernel_spmd(nc, in_maps, core_ids=list(range(8)))
    out = np.empty((2, 8192, D), dtype=np.float32)
    for c in range(8):
        b = c // 4
        s = (c % 4) * TPC
        out[b, s:s + TPC, :] = np.asarray(res.results[c]["o_fm"]).T
    return out


# version 13
# speedup vs baseline: 1.0899x; 1.0899x over previous
import numpy as np
import ml_dtypes
from contextlib import ExitStack

import concourse.bass as bass
import concourse.mybir as mybir
from concourse.bass_utils import run_bass_kernel_spmd

F32 = mybir.dt.float32
BF16 = mybir.dt.bfloat16
AX = mybir.AxisListType.X
AF = mybir.ActivationFunctionType
ALU = mybir.AluOpType

D = 2048
HID = 8192
WIN = 128
TPC = 2048
TT = 512
NT = TPC // TT
DC = D // 128
NG = 4
GD = D // NG
HS = HID // 128
RT = (TT + 128) // 128
UN = 8

BF = ml_dtypes.bfloat16


class Sched:

    def __init__(self):
        self.ops = {e: [] for e in "TVSGY"}
        self.cnt = {}
        for e in "TVS":
            self.cnt[e] = 0

    def op(self, e, fn, inc=0):
        assert e in "TVS"
        self.ops[e].append(("op", fn, inc, e))
        self.cnt[e] += inc
        return self.cnt[e]

    def dma(self, e, stream, fn):
        assert e in "GY"
        self.cnt.setdefault(stream, 0)
        self.ops[e].append(("op", fn, 16, stream))
        self.cnt[stream] += 16
        return self.cnt[stream]

    def wait(self, e, on, val):
        if val and val > 0:
            assert val <= self.cnt[on], (e, on, val, self.cnt[on])
            self.ops[e].append(("wait", on, val))

    def raw_wait(self, e, on, val):
        self.ops[e].append(("wait", on, val))

    def vpad_inc(self):
        self.op("V", lambda v: v.engine_nop())
        return self.op("V", lambda v: v.engine_nop(), inc=1)

    def emit(self, eng, e, sems):
        for entry in self.ops[e]:
            if entry[0] == "wait":
                eng.wait_ge(sems[entry[1]], entry[2])
            else:
                _, fn, inc, semname = entry
                ins = fn(eng)
                if inc:
                    ins.then_inc(sems[semname], inc)


def build_nc(debug=False):
    nc = bass.Bass("TRN2", target_bir_lowering=False)

    x_tok = nc.dram_tensor("x_tok", [TPC + 128, D], F32, kind="ExternalInput").ap()
    x_fm = nc.dram_tensor("x_fm", [D, TPC], F32, kind="ExternalInput").ap()
    wq = nc.dram_tensor("wq", [NG, 128, DC, GD], F32, kind="ExternalInput").ap()
    wk = nc.dram_tensor("wk", [NG, 128, DC, GD], F32, kind="ExternalInput").ap()
    wv = nc.dram_tensor("wv", [NG, 128, DC, GD], F32, kind="ExternalInput").ap()
    wo = nc.dram_tensor("wo", [128, DC, DC, 128], F32, kind="ExternalInput").ap()
    wg = nc.dram_tensor("wg", [128, HS, DC, 128], F32, kind="ExternalInput").ap()
    wu = nc.dram_tensor("wu", [128, HS, DC, 128], F32, kind="ExternalInput").ap()
    wd = nc.dram_tensor("wd", [128, DC, HS, 128], F32, kind="ExternalInput").ap()
    mask_std = nc.dram_tensor("mask_std", [128, 256], BF16, kind="ExternalInput").ap()
    mask_first = nc.dram_tensor("mask_first", [128, 256], BF16, kind="ExternalInput").ap()
    ident_in = nc.dram_tensor("ident", [128, 128], BF16, kind="ExternalInput").ap()
    o_fm = nc.dram_tensor("o_fm", [D, TPC], F32, kind="ExternalOutput").ap()
    dbg = {}
    if debug:
        dbg["xnfm"] = nc.dram_tensor("d_xnfm", [128, DC, TT + 128], BF16, kind="ExternalOutput").ap()
        dbg["qg"] = nc.dram_tensor("d_qg", [128, 4, TT], BF16, kind="ExternalOutput").ap()
        dbg["kg"] = nc.dram_tensor("d_kg", [128, 4, TT + 128], BF16, kind="ExternalOutput").ap()
        dbg["vtm"] = nc.dram_tensor("d_vtm", [128, RT, GD], BF16, kind="ExternalOutput").ap()
        dbg["pb"] = nc.dram_tensor("d_pb", [128, 4, 256], BF16, kind="ExternalOutput").ap()
        dbg["pm"] = nc.dram_tensor("d_pm", [128, 4, 256], BF16, kind="ExternalOutput").ap()
        dbg["heads"] = nc.dram_tensor("d_heads", [128, DC, TT], BF16, kind="ExternalOutput").ap()
        dbg["hbuf"] = nc.dram_tensor("d_hbuf", [128, HS, TT], BF16, kind="ExternalOutput").ap()
        dbg["sums"] = nc.dram_tensor("d_sums", [128, 2, 4], F32, kind="ExternalOutput").ap()

    es = ExitStack()
    sb = lambda n, sh, dt: es.enter_context(nc.sbuf_tensor(n, sh, dt))
    psum = lambda n, sh, dt: es.enter_context(nc.psum_tensor(n, sh, dt))

    xrow = [sb(f"xrow{i}", [128, D], F32) for i in range(2)]
    xn_tok = [sb(f"xntok{i}", [128, D], BF16) for i in range(2)]
    sq_scr = sb("sq_scr", [128, 512], F32)
    st_s1 = sb("st_s1", [128, 5], F32)
    st_s2q = sb("st_s2q", [128, 5, 4], F32)
    st_mean = sb("st_mean", [128, 5], F32)
    st_msq = sb("st_msq", [128, 5], F32)
    st_msq2 = sb("st_msq2", [128, 5], F32)
    st_var = sb("st_var", [128, 5], F32)
    st_rstd = sb("st_rstd", [128, 5], F32)
    st_nbias = sb("st_nbias", [128, 5], F32)
    xn_fm = sb("xn_fm", [128, DC, TT + 128], BF16)
    warena = sb("warena", [128, 20480], BF16)
    qz = [sb(f"qz{i}", [128, 4, TT], BF16) for i in range(2)]
    k_g = sb("k_g", [128, 4, TT + 128], BF16)
    v_tm = sb("v_tm", [128, RT, GD], BF16)
    heads = sb("heads", [128, DC, TT], BF16)
    hbuf = sb("hbuf", [128, HS, TT], BF16)
    silu_t = [sb(f"silu{i}", [128, TT], BF16) for i in range(2)]
    cbias = sb("cbias", [128, 1], F32)
    sumsb = sb("sumsb", [128, 2, 4], F32)
    recipb = sb("recipb", [128, 2, 4], F32)
    pmbuf = [sb(f"pmbuf{i}", [128, 4, 256], BF16) for i in range(2)]
    ptbuf = [sb(f"ptbuf{i}", [128, 1024], BF16) for i in range(2)]
    xfmb = [sb(f"xfmb{i}", [128, TT], F32) for i in range(2)]
    k_halo = [sb(f"khalo{i}", [128, 4, 128], BF16) for i in range(NG)]
    v_halo = [sb(f"vhalo{i}", [128, GD], BF16) for i in range(NG)]
    mskS = sb("mskS", [128, 256], BF16)
    mskF = sb("mskF", [128, 256], BF16)
    ident = sb("identb", [128, 128], BF16)

    def arena(off, shape):
        n = int(np.prod(shape))
        apv = warena[:, off:off + n]
        return apv.rearrange("p (a b) -> p a b", b=shape[1])

    qkv_slot = [arena(0, [DC, GD]), arena(8192, [DC, GD])]
    wg_slot = [arena(0, [DC, 128]), arena(2048, [DC, 128]), arena(4096, [DC, 128])]
    wu_slot = [arena(6144, [DC, 128]), arena(8192, [DC, 128]), arena(10240, [DC, 128])]
    wo_slot = [arena(0, [DC, 128]), arena(2048, [DC, 128])]
    wd_slot = [arena(4096, [HS, 128]), arena(12288, [HS, 128])]

    ps_ab = [psum("ps_a", [128, 512], F32), psum("ps_b", [128, 512], F32)]
    ps_s = [psum("ps_s0", [128, 1024], F32), psum("ps_s1", [128, 1024], F32)]
    ps_pt = [psum("ps_pt0", [128, 1024], BF16), psum("ps_pt1", [128, 1024], BF16)]

    sems = {}
    for e in ["T", "V", "S", "c", "xr0", "xr1", "xf0", "xf1", "st0", "st1",
              "qk0", "qk1", "wgu0", "wgu1", "wgu2", "wo0", "wo1", "wd0", "wd1"]:
        sems[e] = es.enter_context(nc.semaphore(f"sem_{e}"))

    S = Sched()
    ev = {}
    guard = {}

    def rec(key, eng, cnt):
        d = guard.setdefault(key, {})
        d[eng] = max(d.get(eng, 0), cnt)

    def gwait(waiter, key):
        for eng, cnt in guard.get(key, {}).items():
            if eng != waiter:
                S.wait(waiter, eng, cnt)

    S.dma("Y", "c", lambda y: y.dma_start(out=mskS[:], in_=mask_std))
    S.dma("Y", "c", lambda y: y.dma_start(out=mskF[:], in_=mask_first))
    consts = S.dma("Y", "c", lambda y: y.dma_start(out=ident[:], in_=ident_in))
    S.wait("T", "c", consts)
    S.wait("V", "c", consts)

    S.op("V", lambda v: v.memset(qz[0][:], 0.0))
    S.op("V", lambda v: v.memset(qz[1][:], 0.0))
    S.op("V", lambda v: v.memset(cbias[:], -4.0))
    qz_zeroed = S.vpad_inc()
    S.wait("S", "V", qz_zeroed)
    S.wait("T", "V", qz_zeroed)

    arena_T_free = 0

    ln_seq = [0]

    def emit_ln_stats(tl, rt):
        grt = tl * RT + rt
        seq = ln_seq[0]
        ln_seq[0] += 1
        p2 = seq % 2
        sl = seq % 5
        ev[("ln_idx", grt)] = (seq, p2, sl)
        row0 = tl * TT + 128 * rt
        pe = ev.get(("ln_apply_seq", seq - 5))
        if pe is not None:
            S.wait("V", "S", pe)
        gwait("Y", ("xrow", p2))
        ld = S.dma("Y", f"xr{p2}", lambda y, r0=row0, p=p2: y.dma_start(
            out=xrow[p][:], in_=x_tok[r0:r0 + 128, :]))
        S.wait("V", f"xr{p2}", ld)
        vsum = S.op("V", lambda v, p=p2, q=sl: v.reduce_sum(
            st_s1[:, q:q + 1], xrow[p][:], axis=AX), inc=1)
        rec(("xrow", p2), "V", vsum)
        S.wait("S", f"xr{p2}", ld)
        pe = ev.get(("sq_done", grt - 1))
        if pe is not None:
            S.wait("S", "S", pe)
        for qq in range(4):
            sq_done = S.op("S", lambda s, p=p2, w=sl, q=qq: s.activation(
                sq_scr[:], xrow[p][:, 512 * q:512 * (q + 1)], AF.Square,
                accum_out=st_s2q[:, w, q:q + 1]), inc=1)
            if qq < 3:
                S.wait("S", "S", sq_done)
        ev[("sq_done", grt)] = sq_done
        S.wait("V", "S", sq_done)
        c = S.op("V", lambda v, q=sl: v.reduce_sum(
            st_msq[:, q:q + 1], st_s2q[:, q, :], axis=AX), inc=1)
        S.wait("V", "V", c)
        S.op("V", lambda v: v.engine_nop())
        c = S.op("V", lambda v, q=sl: v.tensor_scalar_mul(
            st_mean[:, q:q + 1], st_s1[:, q:q + 1], 1.0 / D), inc=1)
        c = S.op("V", lambda v, q=sl: v.tensor_scalar_mul(
            st_msq[:, q:q + 1], st_msq[:, q:q + 1], 1.0 / D), inc=1)
        S.wait("V", "V", c)
        S.op("V", lambda v: v.engine_nop())
        c = S.op("V", lambda v, q=sl: v.tensor_scalar(
            st_msq2[:, q:q + 1], st_mean[:, q:q + 1], st_mean[:, q:q + 1],
            None, ALU.mult), inc=1)
        S.wait("V", "V", c)
        S.op("V", lambda v: v.engine_nop())
        S.op("V", lambda v: v.engine_nop())
        S.op("V", lambda v, q=sl: v.tensor_scalar(
            st_var[:, q:q + 1], st_msq[:, q:q + 1], st_msq2[:, q:q + 1], 1e-5,
            ALU.subtract, ALU.add))
        var_done = S.vpad_inc()
        S.wait("S", "V", var_done)
        sqrt_done = S.op("S", lambda s, q=sl: s.activation(
            st_rstd[:, q:q + 1], st_var[:, q:q + 1], AF.Sqrt), inc=1)
        S.wait("V", "S", sqrt_done)
        c = S.op("V", lambda v, q=sl: v.reciprocal(
            st_rstd[:, q:q + 1], st_rstd[:, q:q + 1]), inc=1)
        S.wait("V", "V", c)
        S.op("V", lambda v: v.engine_nop())
        S.op("V", lambda v: v.engine_nop())
        S.op("V", lambda v, q=sl: v.tensor_scalar(
            st_nbias[:, q:q + 1], st_mean[:, q:q + 1], st_rstd[:, q:q + 1], -1.0,
            ALU.mult, ALU.mult))
        ev[("nbias", grt)] = S.vpad_inc()

    def emit_ln_app(tl, rt):
        grt = tl * RT + rt
        seq, p2, sl = ev[("ln_idx", grt)]
        S.wait("S", "V", ev[("nbias", grt)])
        gwait("S", ("xn_tok", p2))
        apply_done = S.op("S", lambda s, p=p2, q=sl: s.activation(
            xn_tok[p][:], xrow[p][:], AF.Identity,
            bias=st_nbias[:, q:q + 1], scale=st_rstd[:, q:q + 1]), inc=1)
        rec(("xrow", p2), "S", apply_done)
        ev[("ln_apply", grt)] = apply_done
        ev[("ln_apply_seq", seq)] = apply_done

    def emit_ln_tr(tl, rt):
        grt = tl * RT + rt
        seq, p2, sl = ev[("ln_idx", grt)]
        apply_done = ev[("ln_apply", grt)]
        S.wait("T", "S", apply_done)
        for half in range(2):
            gwait("T", ("pt", half))
            for kk in range(8):
                dinc = half * 8 + kk
                last = S.op("T", lambda t, p=p2, d=dinc, h=half, k=kk: t.transpose(
                    ps_pt[h][:, 128 * k:128 * (k + 1)],
                    xn_tok[p][:, 128 * d:128 * (d + 1)], ident[:]),
                    inc=1 if kk == 7 else 0)
            rec(("xn_tok", p2), "T", last)
            S.wait("S", "T", last)
            tev = S.op("S", lambda s, h=half, r=rt: s.copy(
                xn_fm[:, 8 * h:8 * (h + 1), 128 * r:128 * (r + 1)],
                ps_pt[h].ap().rearrange("p (a b) -> p a b", b=128)), inc=1)
            rec(("pt", half), "S", tev)

    for ti in range(NT):
        if ti == 0:
            emit_ln_stats(0, 0)
            emit_ln_stats(0, 1)
            for rt in range(RT):
                emit_ln_app(0, rt)
                if rt + 2 < RT:
                    emit_ln_stats(0, rt + 2)
                emit_ln_tr(0, rt)
            ev[("ln_done", 0)] = S.cnt["S"]
        ln_done_s = ev[("ln_done", ti)]
        if debug and ti == 0:
            S.wait("Y", "S", ln_done_s)
            S.dma("Y", "c", lambda y: y.dma_start(out=dbg["xnfm"], in_=xn_fm[:]))
            for e in "TVS":
                S.wait(e, "c", S.cnt["c"])

        slab_idx = 0

        def load_slab(src, si):
            pe = ev.get(("slab_free", ti, si - 2))
            if pe is None and si < 2:
                S.wait("G", "T", arena_T_free)
            elif pe is not None:
                S.wait("G", "T", pe)
            return S.dma("G", f"qk{si % 2}", lambda gq, s=src, slot=si % 2: gq.dma_start(
                out=qkv_slot[slot][:], in_=s))

        def proj_mms(sl, which, g, carry):
            if which == "q":
                for ds in range(4):
                    par = ds % 2
                    gwait("T", ("ab", par))
                    for dinc in range(DC):
                        last = S.op("T", lambda t, s=sl, d=ds, k=dinc, p=par: t.matmul(
                            ps_ab[p][:], qkv_slot[s][:, k, 128 * d:128 * (d + 1)],
                            xn_fm[:, k, 128:128 + TT],
                            start=(k == 0), stop=(k == DC - 1)),
                            inc=1 if dinc == DC - 1 else 0)
                    S.wait("S", "T", last)
                    S.op("S", lambda s, d=ds, p=par: s.copy(
                        qz[0][0:64, d, :], ps_ab[p][0:64, :]))
                    e = S.op("S", lambda s, d=ds, p=par: s.copy(
                        qz[1][64:128, d, :], ps_ab[p][64:128, :]), inc=1)
                    rec(("ab", par), "S", e)
            elif which == "k":
                if carry:
                    S.wait("S", "T", ev[("attnT_g", ti - 1 if g == 0 else ti,
                                         (g - 1) % NG)])
                    S.op("S", lambda s, gg=g: s.copy(
                        k_g[:, :, 0:128], k_halo[gg][:]))
                for ds in range(4):
                    par = ds % 2
                    gwait("T", ("ab", par))
                    lo = 128 if carry else 0
                    for dinc in range(DC):
                        S.op("T", lambda t, s=sl, d=ds, k=dinc, p=par, l=lo: t.matmul(
                            ps_ab[p][:], qkv_slot[s][:, k, 128 * d:128 * (d + 1)],
                            xn_fm[:, k, l:l + 512],
                            start=(k == 0), stop=(k == DC - 1)))
                    if not carry:
                        gwait("T", ("ab", 1 - par))
                        for dinc in range(DC):
                            last = S.op("T", lambda t, s=sl, d=ds, k=dinc, p=par: t.matmul(
                                ps_ab[1 - p][:, 0:128],
                                qkv_slot[s][:, k, 128 * d:128 * (d + 1)],
                                xn_fm[:, k, 512:640],
                                start=(k == 0), stop=(k == DC - 1)),
                                inc=1 if dinc == DC - 1 else 0)
                    else:
                        S.ops["T"][-1] = ("op", S.ops["T"][-1][1], 1, "T")
                        S.cnt["T"] += 1
                        last = S.cnt["T"]
                    S.wait("S", "T", last)
                    e1 = S.op("S", lambda s, d=ds, p=par, l=(128 if carry else 0):
                              s.copy(k_g[:, d, l:l + 512], ps_ab[p][:]), inc=1)
                    rec(("ab", par), "S", e1)
                    if not carry:
                        e2 = S.op("S", lambda s, d=ds, p=par: s.copy(
                            k_g[:, d, 512:640], ps_ab[1 - p][:, 0:128]), inc=1)
                        rec(("ab", 1 - par), "S", e2)
                S.op("S", lambda s, gg=g: s.copy(
                    k_halo[gg][:], k_g[:, :, 512:640]))
            else:
                if carry:
                    S.op("S", lambda s, gg=g: s.copy(v_tm[:, 0, :], v_halo[gg][:]))
                for tc in (range(1, RT) if carry else range(RT)):
                    par = tc % 2
                    gwait("T", ("ab", par))
                    for dinc in range(DC):
                        last = S.op("T", lambda t, s=sl, c=tc, k=dinc, p=par: t.matmul(
                            ps_ab[p][:], xn_fm[:, k, 128 * c:128 * (c + 1)],
                            qkv_slot[s][:, k, :],
                            start=(k == 0), stop=(k == DC - 1)),
                            inc=1 if dinc == DC - 1 else 0)
                    S.wait("S", "T", last)
                    e = S.op("S", lambda s, c=tc, p=par: s.copy(
                        v_tm[:, c, :], ps_ab[p][:]), inc=1)
                    rec(("ab", par), "S", e)
                S.op("S", lambda s, gg=g: s.copy(v_halo[gg][:], v_tm[:, 4, :]))

        for g in range(NG):
            ld_q = load_slab(wq[g], slab_idx); sl_q = slab_idx % 2; slab_idx += 1
            ld_k = load_slab(wk[g], slab_idx); sl_k = slab_idx % 2; slab_idx += 1

            S.wait("T", f"qk{sl_q}", ld_q)
            if g == 0:
                S.wait("T", "S", ln_done_s)
            proj_mms(sl_q, "q", g, ti > 0)
            ev[("slab_free", ti, slab_idx - 2)] = S.cnt["T"]
            ld_v = load_slab(wv[g], slab_idx); sl_v = slab_idx % 2; slab_idx += 1
            S.wait("T", f"qk{sl_k}", ld_k)
            proj_mms(sl_k, "k", g, ti > 0)
            ev[("slab_free", ti, slab_idx - 2)] = S.cnt["T"]
            S.wait("T", f"qk{sl_v}", ld_v)
            proj_mms(sl_v, "v", g, ti > 0)
            ev[("slab_free", ti, slab_idx - 1)] = S.cnt["T"]
            if debug and ti == 0 and g == 0:
                S.wait("Y", "S", S.cnt["S"])
                S.dma("Y", "c", lambda y: y.dma_start(out=dbg["qg"], in_=qz[0][:]))
                S.dma("Y", "c", lambda y: y.dma_start(out=dbg["kg"], in_=k_g[:]))
                S.dma("Y", "c", lambda y: y.dma_start(out=dbg["vtm"], in_=v_tm[:]))
                for e in "TVS":
                    S.wait(e, "c", S.cnt["c"])

            gu0 = (ti * NG + g) * UN
            qkv_evacs_done = S.cnt["S"]

            def unit_qb(u, j):
                return 2 * (u % 2) + (j % 2)

            def t1_scores(u):
                gu = gu0 + u
                par = gu % 2
                if u == 0:
                    S.wait("T", "S", qkv_evacs_done)
                gwait("T", ("ss", par))
                hp = u // 2
                for j in range(4):
                    h_in = j // 2
                    qb = unit_qb(u, j)
                    last = S.op("T", lambda t, p=par, hh=hp, hi=h_in, q=qb, jj=j: t.matmul(
                        ps_s[p][:, 256 * jj:256 * (jj + 1)],
                        qz[hi][:, hh, 128 * q:128 * (q + 1)],
                        k_g[:, hh, 128 * q:128 * q + 256],
                        start=True, stop=True),
                        inc=1 if j == 3 else 0)
                ev[("scores", gu)] = last

            def v3_stt(u):
                gu = gu0 + u
                par = gu % 2
                S.wait("V", "S", ev[("exp", gu)])
                pe = ev.get(("pt", gu - 2))
                if pe is not None:
                    S.wait("V", "T", pe)
                for j in range(4):
                    qb = unit_qb(u, j)
                    mk = mskF if (ti == 0 and qb == 0) else mskS
                    c = S.op("V", lambda v, p=par, jj=j, m=mk: v.scalar_tensor_tensor(
                        pmbuf[p][:, jj, :], pmbuf[p][:, jj, :], 1.0, m[:],
                        ALU.mult, ALU.mult,
                        accum_out=sumsb[:, p, jj:jj + 1]), inc=1 if j == 3 else 0)
                ev[("stt", gu)] = c

            def v_mid(u):
                if not (0 <= u - 1 < UN):
                    return
                gp = gu0 + u - 1
                pp = gp % 2
                S.wait("V", "V", ev[("stt", gp)])
                S.op("V", lambda v: v.engine_nop())
                S.op("V", lambda v: v.engine_nop())
                S.op("V", lambda v: v.engine_nop())
                c = S.op("V", lambda v, p=pp: v.reciprocal(
                    recipb[:, p, :], sumsb[:, p, :]), inc=1)
                S.wait("V", "V", c)
                S.op("V", lambda v: v.engine_nop())
                S.op("V", lambda v: v.engine_nop())
                S.op("V", lambda v: v.engine_nop())
                pe = ev.get(("pt", gp - 2))
                if pe is not None:
                    S.wait("V", "T", pe)
                for j in range(4):
                    S.op("V", lambda v, p=pp, jj=j: v.tensor_scalar_mul(
                        pmbuf[p][:, jj, :], pmbuf[p][:, jj, :],
                        recipb[:, p, jj:jj + 1]))
                ev[("pscale", gp)] = S.vpad_inc()

            def v_mid(u):
                if not (0 <= u - 1 < UN):
                    return
                gp = gu0 + u - 1
                pp = gp % 2
                S.wait("V", "V", ev[("stt", gp)])
                S.op("V", lambda v: v.engine_nop())
                S.op("V", lambda v: v.engine_nop())
                S.op("V", lambda v: v.engine_nop())
                c = S.op("V", lambda v, p=pp: v.reciprocal(
                    recipb[:, p, :], sumsb[:, p, :]), inc=1)
                S.wait("V", "V", c)
                S.op("V", lambda v: v.engine_nop())
                S.op("V", lambda v: v.engine_nop())
                S.op("V", lambda v: v.engine_nop())
                pe = ev.get(("pt", gp - 2))
                if pe is not None:
                    S.wait("V", "T", pe)
                for j in range(4):
                    S.op("V", lambda v, p=pp, jj=j: v.tensor_scalar_mul(
                        pmbuf[p][:, jj, :], pmbuf[p][:, jj, :],
                        recipb[:, p, jj:jj + 1]))
                ev[("pscale", gp)] = S.vpad_inc()

            def s1_exp(u):
                gu = gu0 + u
                par = gu % 2
                S.wait("S", "T", ev[("scores", gu)])
                pe = ev.get(("pt", gu - 2))
                if pe is not None:
                    S.wait("S", "T", pe)
                for half in range(2):
                    last = S.op("S", lambda s, p=par, h=half: s.activation(
                        pmbuf[p][:, 2 * h:2 * h + 2, :],
                        ps_s[p][:, 512 * h:512 * (h + 1)].rearrange(
                            "p (a b) -> p a b", b=256),
                        AF.Exp, bias=cbias[:], scale=0.125),
                        inc=1 if half == 1 else 0)
                ev[("exp", gu)] = last
                rec(("ss", par), "S", last)

            def t2_pt(u):
                gu = gu0 + u
                par = gu % 2
                S.wait("T", "V", ev[("pscale", gu)])
                gwait("T", ("pt", par))
                for j in range(4):
                    for cc in range(2):
                        idx = 2 * j + cc
                        last = S.op("T", lambda t, p=par, jj=j, c=cc, i=idx: t.transpose(
                            ps_pt[p][:, 128 * i:128 * (i + 1)],
                            pmbuf[p][:, jj, 128 * c:128 * (c + 1)], ident[:]),
                            inc=1 if idx == 7 else 0)
                ev[("pt", gu)] = last

            def s2_ptevac(u):
                gu = gu0 + u
                par = gu % 2
                S.wait("S", "T", ev[("pt", gu)])
                pe = ev.get(("pv", gu - 2))
                if pe is not None:
                    S.wait("S", "T", pe)
                e = S.op("S", lambda s, p=par: s.copy(ptbuf[p][:], ps_pt[p].ap()), inc=1)
                ev[("ptevac", gu)] = e
                rec(("pt", par), "S", e)

            def t3_pv(u):
                gu = gu0 + u
                par = gu % 2
                hp = u // 2
                S.wait("T", "S", ev[("ptevac", gu)])
                gwait("T", ("ab", par))
                first = True
                for jq in range(2):
                    for cc in range(2):
                        for h_in in range(2):
                            j = 2 * h_in + jq
                            qb = unit_qb(u, j)
                            idx = 2 * j + cc
                            reg = 2 * jq + h_in
                            is_last = (jq, cc, h_in) == (1, 1, 1)
                            last = S.op("T", lambda t, p=par, c=cc, q=qb, hh=hp,
                                        i=idx, r=reg, st=first, sp=is_last: t.matmul(
                                ps_ab[p][:, 128 * r:128 * (r + 1)],
                                v_tm[:, q + c, 128 * hh:128 * (hh + 1)],
                                ptbuf[p][:, 128 * i:128 * (i + 1)],
                                start=st, stop=sp, skip_group_check=True),
                                inc=1 if is_last else 0)
                            first = False
                ev[("pv", gu)] = last

            def s3_headsevac(u):
                gu = gu0 + u
                par = gu % 2
                hp = u // 2
                qh = u % 2
                S.wait("S", "T", ev[("pv", gu)])
                for h_in in range(2):
                    for jq in range(2):
                        e = S.op("S", lambda s, p=par, hh=hp, q=qh, gg=g, hi=h_in,
                                 jj=jq: s.copy(
                            heads[64 * hi:64 * hi + 64, 4 * gg + hh,
                                  256 * q + 128 * jj:256 * q + 128 * jj + 128],
                            ps_ab[p][64 * hi:64 * hi + 64,
                                     128 * (2 * jj + hi):128 * (2 * jj + hi) + 128]),
                            inc=1 if (h_in, jq) == (1, 1) else 0)
                rec(("ab", par), "S", e)

            for u in range(UN + 2):
                if debug and ti == 0 and g == 0 and u == 1:
                    S.wait("Y", "S", ev[("exp", gu0)])
                    S.dma("Y", "c", lambda y: y.dma_start(out=dbg["pb"], in_=pmbuf[0][:]))
                    for e in "TVS":
                        S.wait(e, "c", S.cnt["c"])
                if debug and ti == 0 and g == 0 and u == 2:
                    S.wait("Y", "V", ev[("pscale", gu0)])
                    S.dma("Y", "c", lambda y: y.dma_start(out=dbg["pm"], in_=pmbuf[0][:]))
                    S.dma("Y", "c", lambda y: y.dma_start(out=dbg["sums"], in_=sumsb[:]))
                    for e in "TVS":
                        S.wait(e, "c", S.cnt["c"])
                if u < UN:
                    t1_scores(u)
                if 0 <= u - 1 < UN:
                    v3_stt(u - 1)
                v_mid(u)
                if u < UN:
                    s1_exp(u)
                if 0 <= u - 1 < UN:
                    t2_pt(u - 1)
                    s2_ptevac(u - 1)
                if 0 <= u - 2 < UN:
                    t3_pv(u - 2)
                    s3_headsevac(u - 2)
            ev[("attnT_g", ti, g)] = S.cnt["T"]

        attn_T_done = S.cnt["T"]
        heads_done_s = S.cnt["S"]
        ev[("attnT_g", ti, NG - 1)] = attn_T_done

        for hs in range(HS):
            par = hs % 2
            wsl = hs % 3
            if ti + 1 < NT:
                if hs == 36:
                    emit_ln_stats(ti + 1, 1)
                elif hs == 40:
                    emit_ln_app(ti + 1, 1)
                elif hs == 44:
                    emit_ln_stats(ti + 1, 2)
                elif hs == 48:
                    emit_ln_app(ti + 1, 2)
                elif hs == 52:
                    emit_ln_stats(ti + 1, 3)
                elif hs == 56:
                    emit_ln_stats(ti + 1, 4)
            pe = ev.get(("wgu_free", ti, hs - 3))
            if pe is None and hs < 3:
                S.wait("G", "T", ev[("slab_free", ti, 11)])
            elif pe is not None:
                S.wait("G", "T", pe)
            ld_g = S.dma("G", f"wgu{wsl}", lambda gq, h=hs, p=wsl: gq.dma_start(
                out=wg_slot[p][:], in_=wg[:, h]))
            ld_u = S.dma("G", f"wgu{wsl}", lambda gq, h=hs, p=wsl: gq.dma_start(
                out=wu_slot[p][:], in_=wu[:, h]))
            S.wait("T", f"wgu{wsl}", ld_u)
            gwait("T", ("ss", par))
            for dinc in range(DC):
                S.op("T", lambda t, k=dinc, p=par, w=wsl: t.matmul(
                    ps_s[p][:, 0:512], wg_slot[w][:, k, :], xn_fm[:, k, 128:128 + TT],
                    start=(k == 0), stop=(k == DC - 1)))
            for dinc in range(DC):
                last = S.op("T", lambda t, k=dinc, p=par, w=wsl: t.matmul(
                    ps_s[p][:, 512:1024], wu_slot[w][:, k, :], xn_fm[:, k, 128:128 + TT],
                    start=(k == 0), stop=(k == DC - 1)),
                    inc=1 if dinc == DC - 1 else 0)
            ev[("wgu_free", ti, hs)] = last
            S.wait("S", "T", last)
            sil = S.op("S", lambda s, p=par: s.activation(
                silu_t[p][:], ps_s[p][:, 0:512], AF.Silu), inc=1)
            rec(("ss", par), "S", sil)
            S.wait("V", "S", sil)
            hm = S.op("V", lambda v, h=hs, p=par: v.tensor_mul(
                hbuf[:, h, :], silu_t[p][:], ps_s[p][:, 512:1024]), inc=1)
            rec(("ss", par), "V", hm)
        ffn_h_done_v = S.vpad_inc()
        gate_T_done = S.cnt["T"]
        if debug and ti == 0:
            S.wait("Y", "S", heads_done_s)
            S.dma("Y", "c", lambda y: y.dma_start(out=dbg["heads"], in_=heads[:]))
            S.wait("Y", "V", ffn_h_done_v)
            S.dma("Y", "c", lambda y: y.dma_start(out=dbg["hbuf"], in_=hbuf[:]))
            for e in "TVS":
                S.wait(e, "c", S.cnt["c"])

        for ds in range(DC):
            par = ds % 2
            osl = (ds + 1) % 2
            pe = ev.get(("wod_free", ti, ds - 2))
            if pe is not None:
                S.wait("G", "T", pe)
            elif ds == 0:
                S.wait("G", "T", ev[("slab_free", ti, 11)])
            elif ds == 1:
                S.wait("G", "T", ev[("wgu_free", ti, HS - 1)])
            ld_d = S.dma("G", f"wd{osl}", lambda gq, d=ds, p=osl: gq.dma_start(
                out=wd_slot[p][:], in_=wd[:, d]))
            if ds < 2:
                S.wait("G", "T", ev[("wgu_free", ti, HS - 1 if ds == 0 else HS - 3)])
            ld_o = S.dma("G", f"wo{par}", lambda gq, d=ds, p=par: gq.dma_start(
                out=wo_slot[p][:], in_=wo[:, d]))
            pst_x = ev.get(("store", ti, ds - 2))
            if pst_x is None and ds < 2:
                pst_x = ev.get(("store", ti - 1, DC - 2 + ds))
            if pst_x is not None:
                S.wait("Y", f"st{par}", pst_x)
            ld_x = S.dma("Y", f"xf{par}", lambda y, d=ds, t=ti, p=par: y.dma_start(
                out=xfmb[p][:], in_=x_fm[128 * d:128 * (d + 1), TT * t:TT * (t + 1)]))
            S.wait("T", f"wd{osl}", ld_d)
            S.wait("T", f"wo{par}", ld_o)
            if ds == 0:
                S.wait("T", "V", ffn_h_done_v)
                S.wait("T", "S", heads_done_s)
            gwait("T", ("ab", par))
            for hdc in range(DC):
                S.op("T", lambda t, k=hdc, p=par: t.matmul(
                    ps_ab[p][:], wo_slot[p][:, k, :], heads[:, k, :],
                    start=(k == 0), stop=False))
            for hc in range(HS):
                last = S.op("T", lambda t, k=hc, p=par, o=osl: t.matmul(
                    ps_ab[p][:], wd_slot[o][:, k, :], hbuf[:, k, :],
                    start=False, stop=(k == HS - 1)),
                    inc=1 if hc == HS - 1 else 0)
            ev[("wod_free", ti, ds)] = last
            S.wait("V", "T", last)
            S.wait("V", f"xf{par}", ld_x)
            pst = ev.get(("store", ti, ds - 2))
            if pst is None and ds < 2:
                pst = ev.get(("store", ti - 1, DC - 2 + ds))
            if pst is not None:
                S.wait("V", f"st{par}", pst)
            S.op("V", lambda v, p=par: v.tensor_add(
                xfmb[p][:], ps_ab[p][:], xfmb[p][:]))
            oe = S.vpad_inc()
            ev[("outevac", ti, ds)] = oe
            rec(("ab", par), "V", oe)
            S.wait("Y", "V", oe)
            ev[("store", ti, ds)] = S.dma("Y", f"st{par}", lambda y, d=ds, t=ti, p=par: y.dma_start(
                out=o_fm[128 * d:128 * (d + 1), TT * t:TT * (t + 1)], in_=xfmb[p][:]))
        if ti + 1 < NT:
            emit_ln_tr(ti + 1, 1)
            emit_ln_app(ti + 1, 3)
            emit_ln_tr(ti + 1, 2)
            emit_ln_app(ti + 1, 4)
            emit_ln_tr(ti + 1, 3)
            emit_ln_tr(ti + 1, 4)
            ev[("ln_done", ti + 1)] = S.cnt["S"]
        arena_T_free = S.cnt["T"]

    for sname in ["c", "xr0", "xr1", "xf0", "xf1", "st0", "st1"]:
        if S.cnt.get(sname):
            S.raw_wait("Y", sname, S.cnt[sname])
    for sname in ["qk0", "qk1", "wgu0", "wgu1", "wgu2", "wo0", "wo1", "wd0", "wd1"]:
        if S.cnt.get(sname):
            S.raw_wait("G", sname, S.cnt[sname])

    with nc.Block() as block:
        @block.tensor
        def _(t):
            S.emit(t, "T", sems)

        @block.vector
        def _(v):
            S.emit(v, "V", sems)

        @block.scalar
        def _(s):
            S.emit(s, "S", sems)

        @block.gpsimd
        def _(gq):
            S.emit(gq, "G", sems)

        @block.sync
        def _(y):
            S.emit(y, "Y", sems)

    es.close()
    return nc


def _prep_weights(Wq, Wk, Wv, Wo, Wg, Wu, Wd):
    def qkv_layout(W):
        return np.ascontiguousarray(W.reshape(NG, GD, DC, 128).transpose(0, 3, 2, 1))

    return (qkv_layout(Wq), qkv_layout(Wk), qkv_layout(Wv),
            np.ascontiguousarray(Wo.reshape(DC, 128, DC, 128).transpose(3, 0, 2, 1)),
            np.ascontiguousarray(Wg.reshape(HS, 128, DC, 128).transpose(3, 0, 2, 1)),
            np.ascontiguousarray(Wu.reshape(HS, 128, DC, 128).transpose(3, 0, 2, 1)),
            np.ascontiguousarray(Wd.reshape(DC, 128, HS, 128).transpose(3, 0, 2, 1)))


def _masks():
    r = np.arange(128)[:, None]
    c = np.arange(256)[None, :]
    band = (r <= c) & (c <= r + WIN)
    m_std = band.astype(np.float32).astype(BF)
    m_first = (band & (c >= 128)).astype(np.float32).astype(BF)
    return m_std, m_first


def kernel(L0_attn_x, L0_attn_Wq, L0_attn_Wk, L0_attn_Wv, L0_attn_Wo,
           L0_ffn_w_gate, L0_ffn_w_up, L0_ffn_w_down):
    x = np.asarray(L0_attn_x)
    wq, wk, wv, wo, wg, wu, wd = _prep_weights(
        np.asarray(L0_attn_Wq), np.asarray(L0_attn_Wk), np.asarray(L0_attn_Wv),
        np.asarray(L0_attn_Wo), np.asarray(L0_ffn_w_gate), np.asarray(L0_ffn_w_up),
        np.asarray(L0_ffn_w_down))
    m_std, m_first = _masks()
    ident = np.eye(128, dtype=BF)

    nc = build_nc()
    in_maps = []
    for c in range(8):
        b = c // 4
        s = (c % 4) * TPC
        if s == 0:
            halo = np.zeros((128, D), dtype=np.float32)
            mf = m_first
        else:
            halo = x[b, s - 128:s, :]
            mf = m_std
        x_tok = np.ascontiguousarray(np.concatenate([halo, x[b, s:s + TPC, :]], axis=0))
        x_fm = np.ascontiguousarray(x[b, s:s + TPC, :].T)
        in_maps.append({
            "x_tok": x_tok, "x_fm": x_fm,
            "wq": wq, "wk": wk, "wv": wv, "wo": wo,
            "wg": wg, "wu": wu, "wd": wd,
            "mask_std": m_std, "mask_first": mf, "ident": ident,
        })

    res = run_bass_kernel_spmd(nc, in_maps, core_ids=list(range(8)))
    out = np.empty((2, 8192, D), dtype=np.float32)
    for c in range(8):
        b = c // 4
        s = (c % 4) * TPC
        out[b, s:s + TPC, :] = np.asarray(res.results[c]["o_fm"]).T
    return out


# version 14 (final)
# speedup vs baseline: 1.1551x; 1.0598x over previous
import numpy as np
import ml_dtypes
from contextlib import ExitStack

import concourse.bass as bass
import concourse.mybir as mybir
from concourse.bass_utils import run_bass_kernel_spmd

F32 = mybir.dt.float32
BF16 = mybir.dt.bfloat16
AX = mybir.AxisListType.X
AF = mybir.ActivationFunctionType
ALU = mybir.AluOpType

D = 2048
HID = 8192
WIN = 128
TPC = 2048
TT = 512
NT = TPC // TT
DC = D // 128
NG = 4
GD = D // NG
HS = HID // 128
RT = (TT + 128) // 128
UN = 8

BF = ml_dtypes.bfloat16


class Sched:

    def __init__(self):
        self.ops = {e: [] for e in "TVSGY"}
        self.cnt = {}
        for e in "TVS":
            self.cnt[e] = 0

    def op(self, e, fn, inc=0):
        assert e in "TVS"
        self.ops[e].append(("op", fn, inc, e))
        self.cnt[e] += inc
        return self.cnt[e]

    def dma(self, e, stream, fn):
        assert e in "GY"
        self.cnt.setdefault(stream, 0)
        self.ops[e].append(("op", fn, 16, stream))
        self.cnt[stream] += 16
        return self.cnt[stream]

    def wait(self, e, on, val):
        if val and val > 0:
            assert val <= self.cnt[on], (e, on, val, self.cnt[on])
            self.ops[e].append(("wait", on, val))

    def raw_wait(self, e, on, val):
        self.ops[e].append(("wait", on, val))

    def vpad_inc(self):
        self.op("V", lambda v: v.engine_nop())
        return self.op("V", lambda v: v.engine_nop(), inc=1)

    def emit(self, eng, e, sems):
        for entry in self.ops[e]:
            if entry[0] == "wait":
                eng.wait_ge(sems[entry[1]], entry[2])
            else:
                _, fn, inc, semname = entry
                ins = fn(eng)
                if inc:
                    ins.then_inc(sems[semname], inc)


def build_nc(debug=False):
    nc = bass.Bass("TRN2", target_bir_lowering=False)

    x_tok = nc.dram_tensor("x_tok", [TPC + 128, D], F32, kind="ExternalInput").ap()
    x_fm = nc.dram_tensor("x_fm", [D, TPC], F32, kind="ExternalInput").ap()
    wq = nc.dram_tensor("wq", [NG, 128, DC, GD], F32, kind="ExternalInput").ap()
    wk = nc.dram_tensor("wk", [NG, 128, DC, GD], F32, kind="ExternalInput").ap()
    wv = nc.dram_tensor("wv", [NG, 128, DC, GD], F32, kind="ExternalInput").ap()
    wo = nc.dram_tensor("wo", [128, DC, DC, 128], F32, kind="ExternalInput").ap()
    wg = nc.dram_tensor("wg", [128, HS, DC, 128], F32, kind="ExternalInput").ap()
    wu = nc.dram_tensor("wu", [128, HS, DC, 128], F32, kind="ExternalInput").ap()
    wd = nc.dram_tensor("wd", [128, DC, HS, 128], F32, kind="ExternalInput").ap()
    mask_std = nc.dram_tensor("mask_std", [128, 256], BF16, kind="ExternalInput").ap()
    mask_first = nc.dram_tensor("mask_first", [128, 256], BF16, kind="ExternalInput").ap()
    ident_in = nc.dram_tensor("ident", [128, 128], BF16, kind="ExternalInput").ap()
    o_fm = nc.dram_tensor("o_fm", [D, TPC], F32, kind="ExternalOutput").ap()
    dbg = {}
    if debug:
        dbg["xnfm"] = nc.dram_tensor("d_xnfm", [128, DC, TT + 128], BF16, kind="ExternalOutput").ap()
        dbg["qg"] = nc.dram_tensor("d_qg", [128, 4, TT], BF16, kind="ExternalOutput").ap()
        dbg["kg"] = nc.dram_tensor("d_kg", [128, 4, TT + 128], BF16, kind="ExternalOutput").ap()
        dbg["vtm"] = nc.dram_tensor("d_vtm", [128, RT, GD], BF16, kind="ExternalOutput").ap()
        dbg["pb"] = nc.dram_tensor("d_pb", [128, 4, 256], BF16, kind="ExternalOutput").ap()
        dbg["pm"] = nc.dram_tensor("d_pm", [128, 4, 256], BF16, kind="ExternalOutput").ap()
        dbg["heads"] = nc.dram_tensor("d_heads", [128, DC, TT], BF16, kind="ExternalOutput").ap()
        dbg["hbuf"] = nc.dram_tensor("d_hbuf", [128, HS, TT], BF16, kind="ExternalOutput").ap()
        dbg["sums"] = nc.dram_tensor("d_sums", [128, 2, 4], F32, kind="ExternalOutput").ap()

    es = ExitStack()
    sb = lambda n, sh, dt: es.enter_context(nc.sbuf_tensor(n, sh, dt))
    psum = lambda n, sh, dt: es.enter_context(nc.psum_tensor(n, sh, dt))

    xrow = [sb(f"xrow{i}", [128, D], F32) for i in range(2)]
    xn_tok = [sb(f"xntok{i}", [128, D], BF16) for i in range(2)]
    sq_scr = sb("sq_scr", [128, 512], F32)
    st_s1 = sb("st_s1", [128, 5], F32)
    st_s2q = sb("st_s2q", [128, 5, 4], F32)
    st_mean = sb("st_mean", [128, 5], F32)
    st_msq = sb("st_msq", [128, 5], F32)
    st_msq2 = sb("st_msq2", [128, 5], F32)
    st_var = sb("st_var", [128, 5], F32)
    st_rstd = sb("st_rstd", [128, 5], F32)
    st_nbias = sb("st_nbias", [128, 5], F32)
    xn_fm = sb("xn_fm", [128, DC, TT + 128], BF16)
    warena = sb("warena", [128, 20480], BF16)
    qz = [sb(f"qz{i}", [128, 4, TT], BF16) for i in range(2)]
    k_g = sb("k_g", [128, 4, TT + 128], BF16)
    v_tm = sb("v_tm", [128, RT, GD], BF16)
    heads = sb("heads", [128, DC, TT], BF16)
    hbuf = sb("hbuf", [128, HS, TT], BF16)
    silu_t = [sb(f"silu{i}", [128, TT], BF16) for i in range(2)]
    cbias = sb("cbias", [128, 1], F32)
    sumsb = sb("sumsb", [128, 2, 4], F32)
    recipb = sb("recipb", [128, 2, 4], F32)
    pmbuf = [sb(f"pmbuf{i}", [128, 4, 256], BF16) for i in range(2)]
    ptbuf = [sb(f"ptbuf{i}", [128, 1024], BF16) for i in range(2)]
    xfmb = [sb(f"xfmb{i}", [128, TT], F32) for i in range(2)]
    k_halo = [sb(f"khalo{i}", [128, 4, 128], BF16) for i in range(NG)]
    v_halo = [sb(f"vhalo{i}", [128, GD], BF16) for i in range(NG)]
    mskS = sb("mskS", [128, 256], BF16)
    mskF = sb("mskF", [128, 256], BF16)
    ident = sb("identb", [128, 128], BF16)

    def arena(off, shape):
        n = int(np.prod(shape))
        apv = warena[:, off:off + n]
        return apv.rearrange("p (a b) -> p a b", b=shape[1])

    qkv_slot = [arena(0, [DC, GD]), arena(8192, [DC, GD])]
    wg_slot = [arena(0, [DC, 128]), arena(2048, [DC, 128]), arena(4096, [DC, 128])]
    wu_slot = [arena(6144, [DC, 128]), arena(8192, [DC, 128]), arena(10240, [DC, 128])]
    wo_slot = [arena(0, [DC, 128]), arena(2048, [DC, 128])]
    wd_slot = [arena(4096, [HS, 128]), arena(12288, [HS, 128])]

    ps_ab = [psum("ps_a", [128, 512], F32), psum("ps_b", [128, 512], F32)]
    ps_s = [psum("ps_s0", [128, 1024], F32), psum("ps_s1", [128, 1024], F32)]
    ps_pt = [psum("ps_pt0", [128, 1024], BF16), psum("ps_pt1", [128, 1024], BF16)]

    sems = {}
    for e in ["T", "V", "S", "c", "xr0", "xr1", "xf0", "xf1", "st0", "st1",
              "qk0", "qk1", "wgu0", "wgu1", "wgu2", "wo0", "wo1", "wd0", "wd1"]:
        sems[e] = es.enter_context(nc.semaphore(f"sem_{e}"))

    S = Sched()
    ev = {}
    guard = {}

    def rec(key, eng, cnt):
        d = guard.setdefault(key, {})
        d[eng] = max(d.get(eng, 0), cnt)

    def gwait(waiter, key):
        for eng, cnt in guard.get(key, {}).items():
            if eng != waiter:
                S.wait(waiter, eng, cnt)

    S.dma("Y", "c", lambda y: y.dma_start(out=mskS[:], in_=mask_std))
    S.dma("Y", "c", lambda y: y.dma_start(out=mskF[:], in_=mask_first))
    consts = S.dma("Y", "c", lambda y: y.dma_start(out=ident[:], in_=ident_in))
    S.wait("T", "c", consts)
    S.wait("V", "c", consts)

    S.op("V", lambda v: v.memset(qz[0][:], 0.0))
    S.op("V", lambda v: v.memset(qz[1][:], 0.0))
    S.op("V", lambda v: v.memset(cbias[:], -4.0))
    qz_zeroed = S.vpad_inc()
    S.wait("S", "V", qz_zeroed)
    S.wait("T", "V", qz_zeroed)

    arena_T_free = 0

    ln_seq = [0]

    def emit_ln_stats(tl, rt):
        grt = tl * RT + rt
        seq = ln_seq[0]
        ln_seq[0] += 1
        p2 = seq % 2
        sl = seq % 5
        ev[("ln_idx", grt)] = (seq, p2, sl)
        row0 = tl * TT + 128 * rt
        pe = ev.get(("ln_apply_seq", seq - 5))
        if pe is not None:
            S.wait("V", "S", pe)
        gwait("Y", ("xrow", p2))
        ld = S.dma("Y", f"xr{p2}", lambda y, r0=row0, p=p2: y.dma_start(
            out=xrow[p][:], in_=x_tok[r0:r0 + 128, :]))
        S.wait("V", f"xr{p2}", ld)
        vsum = S.op("V", lambda v, p=p2, q=sl: v.reduce_sum(
            st_s1[:, q:q + 1], xrow[p][:], axis=AX), inc=1)
        rec(("xrow", p2), "V", vsum)
        S.wait("S", f"xr{p2}", ld)
        pe = ev.get(("sq_done", grt - 1))
        if pe is not None:
            S.wait("S", "S", pe)
        for qq in range(4):
            sq_done = S.op("S", lambda s, p=p2, w=sl, q=qq: s.activation(
                sq_scr[:], xrow[p][:, 512 * q:512 * (q + 1)], AF.Square,
                accum_out=st_s2q[:, w, q:q + 1]), inc=1)
            if qq < 3:
                S.wait("S", "S", sq_done)
        ev[("sq_done", grt)] = sq_done
        S.wait("V", "S", sq_done)
        c = S.op("V", lambda v, q=sl: v.reduce_sum(
            st_msq[:, q:q + 1], st_s2q[:, q, :], axis=AX), inc=1)
        S.wait("V", "V", c)
        S.op("V", lambda v: v.engine_nop())
        c = S.op("V", lambda v, q=sl: v.tensor_scalar_mul(
            st_mean[:, q:q + 1], st_s1[:, q:q + 1], 1.0 / D), inc=1)
        c = S.op("V", lambda v, q=sl: v.tensor_scalar_mul(
            st_msq[:, q:q + 1], st_msq[:, q:q + 1], 1.0 / D), inc=1)
        S.wait("V", "V", c)
        S.op("V", lambda v: v.engine_nop())
        c = S.op("V", lambda v, q=sl: v.tensor_scalar(
            st_msq2[:, q:q + 1], st_mean[:, q:q + 1], st_mean[:, q:q + 1],
            None, ALU.mult), inc=1)
        S.wait("V", "V", c)
        S.op("V", lambda v: v.engine_nop())
        S.op("V", lambda v: v.engine_nop())
        S.op("V", lambda v, q=sl: v.tensor_scalar(
            st_var[:, q:q + 1], st_msq[:, q:q + 1], st_msq2[:, q:q + 1], 1e-5,
            ALU.subtract, ALU.add))
        var_done = S.vpad_inc()
        S.wait("S", "V", var_done)
        sqrt_done = S.op("S", lambda s, q=sl: s.activation(
            st_rstd[:, q:q + 1], st_var[:, q:q + 1], AF.Sqrt), inc=1)
        S.wait("V", "S", sqrt_done)
        c = S.op("V", lambda v, q=sl: v.reciprocal(
            st_rstd[:, q:q + 1], st_rstd[:, q:q + 1]), inc=1)
        S.wait("V", "V", c)
        S.op("V", lambda v: v.engine_nop())
        S.op("V", lambda v: v.engine_nop())
        S.op("V", lambda v, q=sl: v.tensor_scalar(
            st_nbias[:, q:q + 1], st_mean[:, q:q + 1], st_rstd[:, q:q + 1], -1.0,
            ALU.mult, ALU.mult))
        ev[("nbias", grt)] = S.vpad_inc()

    def emit_ln_app(tl, rt):
        grt = tl * RT + rt
        seq, p2, sl = ev[("ln_idx", grt)]
        S.wait("S", "V", ev[("nbias", grt)])
        gwait("S", ("xn_tok", p2))
        apply_done = S.op("S", lambda s, p=p2, q=sl: s.activation(
            xn_tok[p][:], xrow[p][:], AF.Identity,
            bias=st_nbias[:, q:q + 1], scale=st_rstd[:, q:q + 1]), inc=1)
        rec(("xrow", p2), "S", apply_done)
        ev[("ln_apply", grt)] = apply_done
        ev[("ln_apply_seq", seq)] = apply_done

    def emit_ln_tr(tl, rt):
        grt = tl * RT + rt
        seq, p2, sl = ev[("ln_idx", grt)]
        apply_done = ev[("ln_apply", grt)]
        S.wait("T", "S", apply_done)
        for half in range(2):
            gwait("T", ("pt", half))
            for kk in range(8):
                dinc = half * 8 + kk
                last = S.op("T", lambda t, p=p2, d=dinc, h=half, k=kk: t.transpose(
                    ps_pt[h][:, 128 * k:128 * (k + 1)],
                    xn_tok[p][:, 128 * d:128 * (d + 1)], ident[:]),
                    inc=1 if kk == 7 else 0)
            rec(("xn_tok", p2), "T", last)
            S.wait("S", "T", last)
            tev = S.op("S", lambda s, h=half, r=rt: s.copy(
                xn_fm[:, 8 * h:8 * (h + 1), 128 * r:128 * (r + 1)],
                ps_pt[h].ap().rearrange("p (a b) -> p a b", b=128)), inc=1)
            rec(("pt", half), "S", tev)

    for ti in range(NT):
        if ti == 0:
            emit_ln_stats(0, 0)
            emit_ln_stats(0, 1)
            for rt in range(RT):
                emit_ln_app(0, rt)
                if rt + 2 < RT:
                    emit_ln_stats(0, rt + 2)
                emit_ln_tr(0, rt)
            ev[("ln_done", 0)] = S.cnt["S"]
        ln_done_s = ev[("ln_done", ti)]
        if debug and ti == 0:
            S.wait("Y", "S", ln_done_s)
            S.dma("Y", "c", lambda y: y.dma_start(out=dbg["xnfm"], in_=xn_fm[:]))
            for e in "TVS":
                S.wait(e, "c", S.cnt["c"])

        slab_idx = 0

        def load_slab(src, si):
            pe = ev.get(("slab_free", ti, si - 2))
            if pe is None and si < 2:
                S.wait("G", "T", arena_T_free)
            elif pe is not None:
                S.wait("G", "T", pe)
            return S.dma("G", f"qk{si % 2}", lambda gq, s=src, slot=si % 2: gq.dma_start(
                out=qkv_slot[slot][:], in_=s))

        def proj_mms(sl, which, g, carry):
            if which == "q":
                for ds in range(4):
                    par = ds % 2
                    gwait("T", ("ab", par))
                    for dinc in range(DC):
                        last = S.op("T", lambda t, s=sl, d=ds, k=dinc, p=par: t.matmul(
                            ps_ab[p][:], qkv_slot[s][:, k, 128 * d:128 * (d + 1)],
                            xn_fm[:, k, 128:128 + TT],
                            start=(k == 0), stop=(k == DC - 1)),
                            inc=1 if dinc == DC - 1 else 0)
                    S.wait("S", "T", last)
                    S.op("S", lambda s, d=ds, p=par: s.copy(
                        qz[0][0:64, d, :], ps_ab[p][0:64, :]))
                    e = S.op("S", lambda s, d=ds, p=par: s.copy(
                        qz[1][64:128, d, :], ps_ab[p][64:128, :]), inc=1)
                    rec(("ab", par), "S", e)
            elif which == "k":
                if carry:
                    S.wait("S", "T", ev[("attnT_g", ti - 1 if g == 0 else ti,
                                         (g - 1) % NG)])
                    S.wait("S", "S", S.cnt["S"])
                    S.op("S", lambda s, gg=g: s.copy(
                        k_g[:, :, 0:128], k_halo[gg][:]), inc=1)
                for ds in range(4):
                    par = ds % 2
                    gwait("T", ("ab", par))
                    lo = 128 if carry else 0
                    for dinc in range(DC):
                        S.op("T", lambda t, s=sl, d=ds, k=dinc, p=par, l=lo: t.matmul(
                            ps_ab[p][:], qkv_slot[s][:, k, 128 * d:128 * (d + 1)],
                            xn_fm[:, k, l:l + 512],
                            start=(k == 0), stop=(k == DC - 1)))
                    if not carry:
                        gwait("T", ("ab", 1 - par))
                        for dinc in range(DC):
                            last = S.op("T", lambda t, s=sl, d=ds, k=dinc, p=par: t.matmul(
                                ps_ab[1 - p][:, 0:128],
                                qkv_slot[s][:, k, 128 * d:128 * (d + 1)],
                                xn_fm[:, k, 512:640],
                                start=(k == 0), stop=(k == DC - 1)),
                                inc=1 if dinc == DC - 1 else 0)
                    else:
                        S.ops["T"][-1] = ("op", S.ops["T"][-1][1], 1, "T")
                        S.cnt["T"] += 1
                        last = S.cnt["T"]
                    S.wait("S", "T", last)
                    e1 = S.op("S", lambda s, d=ds, p=par, l=(128 if carry else 0):
                              s.copy(k_g[:, d, l:l + 512], ps_ab[p][:]), inc=1)
                    rec(("ab", par), "S", e1)
                    if not carry:
                        e2 = S.op("S", lambda s, d=ds, p=par: s.copy(
                            k_g[:, d, 512:640], ps_ab[1 - p][:, 0:128]), inc=1)
                        rec(("ab", 1 - par), "S", e2)
                S.wait("S", "S", S.cnt["S"])
                S.op("S", lambda s, gg=g: s.copy(
                    k_halo[gg][:], k_g[:, :, 512:640]), inc=1)
            else:
                if carry:
                    S.wait("S", "S", S.cnt["S"])
                    S.op("S", lambda s, gg=g: s.copy(v_tm[:, 0, :], v_halo[gg][:]), inc=1)
                for tc in (range(1, RT) if carry else range(RT)):
                    par = tc % 2
                    gwait("T", ("ab", par))
                    for dinc in range(DC):
                        last = S.op("T", lambda t, s=sl, c=tc, k=dinc, p=par: t.matmul(
                            ps_ab[p][:], xn_fm[:, k, 128 * c:128 * (c + 1)],
                            qkv_slot[s][:, k, :],
                            start=(k == 0), stop=(k == DC - 1)),
                            inc=1 if dinc == DC - 1 else 0)
                    S.wait("S", "T", last)
                    e = S.op("S", lambda s, c=tc, p=par: s.copy(
                        v_tm[:, c, :], ps_ab[p][:]), inc=1)
                    rec(("ab", par), "S", e)
                S.wait("S", "S", S.cnt["S"])
                S.op("S", lambda s, gg=g: s.copy(v_halo[gg][:], v_tm[:, 4, :]), inc=1)

        for g in range(NG):
            ld_q = load_slab(wq[g], slab_idx); sl_q = slab_idx % 2; slab_idx += 1
            ld_k = load_slab(wk[g], slab_idx); sl_k = slab_idx % 2; slab_idx += 1

            S.wait("T", f"qk{sl_q}", ld_q)
            if g == 0:
                S.wait("T", "S", ln_done_s)
            proj_mms(sl_q, "q", g, ti > 0)
            ev[("slab_free", ti, slab_idx - 2)] = S.cnt["T"]
            ld_v = load_slab(wv[g], slab_idx); sl_v = slab_idx % 2; slab_idx += 1
            S.wait("T", f"qk{sl_k}", ld_k)
            proj_mms(sl_k, "k", g, ti > 0)
            ev[("slab_free", ti, slab_idx - 2)] = S.cnt["T"]
            S.wait("T", f"qk{sl_v}", ld_v)
            proj_mms(sl_v, "v", g, ti > 0)
            ev[("slab_free", ti, slab_idx - 1)] = S.cnt["T"]
            if debug and ti == 0 and g == 0:
                S.wait("Y", "S", S.cnt["S"])
                S.dma("Y", "c", lambda y: y.dma_start(out=dbg["qg"], in_=qz[0][:]))
                S.dma("Y", "c", lambda y: y.dma_start(out=dbg["kg"], in_=k_g[:]))
                S.dma("Y", "c", lambda y: y.dma_start(out=dbg["vtm"], in_=v_tm[:]))
                for e in "TVS":
                    S.wait(e, "c", S.cnt["c"])

            gu0 = (ti * NG + g) * UN
            qkv_evacs_done = S.cnt["S"]

            def unit_qb(u, j):
                return 2 * (u % 2) + (j % 2)

            def t1_scores(u):
                gu = gu0 + u
                par = gu % 2
                if u == 0:
                    S.wait("T", "S", qkv_evacs_done)
                gwait("T", ("ss", par))
                hp = u // 2
                for j in range(4):
                    h_in = j // 2
                    qb = unit_qb(u, j)
                    last = S.op("T", lambda t, p=par, hh=hp, hi=h_in, q=qb, jj=j: t.matmul(
                        ps_s[p][:, 256 * jj:256 * (jj + 1)],
                        qz[hi][:, hh, 128 * q:128 * (q + 1)],
                        k_g[:, hh, 128 * q:128 * q + 256],
                        start=True, stop=True),
                        inc=1 if j == 3 else 0)
                ev[("scores", gu)] = last

            def v3_stt(u):
                gu = gu0 + u
                par = gu % 2
                S.wait("V", "S", ev[("exp", gu)])
                pe = ev.get(("pt", gu - 2))
                if pe is not None:
                    S.wait("V", "T", pe)
                for j in range(4):
                    qb = unit_qb(u, j)
                    mk = mskF if (ti == 0 and qb == 0) else mskS
                    c = S.op("V", lambda v, p=par, jj=j, m=mk: v.scalar_tensor_tensor(
                        pmbuf[p][:, jj, :], pmbuf[p][:, jj, :], 1.0, m[:],
                        ALU.mult, ALU.mult,
                        accum_out=sumsb[:, p, jj:jj + 1]), inc=1 if j == 3 else 0)
                ev[("stt", gu)] = c

            def v_mid(u):
                if not (0 <= u - 1 < UN):
                    return
                gp = gu0 + u - 1
                pp = gp % 2
                S.wait("V", "V", ev[("stt", gp)])
                S.op("V", lambda v: v.engine_nop())
                S.op("V", lambda v: v.engine_nop())
                S.op("V", lambda v: v.engine_nop())
                c = S.op("V", lambda v, p=pp: v.reciprocal(
                    recipb[:, p, :], sumsb[:, p, :]), inc=1)
                S.wait("V", "V", c)
                S.op("V", lambda v: v.engine_nop())
                S.op("V", lambda v: v.engine_nop())
                S.op("V", lambda v: v.engine_nop())
                pe = ev.get(("pt", gp - 2))
                if pe is not None:
                    S.wait("V", "T", pe)
                for j in range(4):
                    S.op("V", lambda v, p=pp, jj=j: v.tensor_scalar_mul(
                        pmbuf[p][:, jj, :], pmbuf[p][:, jj, :],
                        recipb[:, p, jj:jj + 1]))
                ev[("pscale", gp)] = S.vpad_inc()

            def v_mid(u):
                if not (0 <= u - 1 < UN):
                    return
                gp = gu0 + u - 1
                pp = gp % 2
                S.wait("V", "V", ev[("stt", gp)])
                S.op("V", lambda v: v.engine_nop())
                S.op("V", lambda v: v.engine_nop())
                S.op("V", lambda v: v.engine_nop())
                c = S.op("V", lambda v, p=pp: v.reciprocal(
                    recipb[:, p, :], sumsb[:, p, :]), inc=1)
                S.wait("V", "V", c)
                S.op("V", lambda v: v.engine_nop())
                S.op("V", lambda v: v.engine_nop())
                S.op("V", lambda v: v.engine_nop())
                pe = ev.get(("pt", gp - 2))
                if pe is not None:
                    S.wait("V", "T", pe)
                for j in range(4):
                    S.op("V", lambda v, p=pp, jj=j: v.tensor_scalar_mul(
                        pmbuf[p][:, jj, :], pmbuf[p][:, jj, :],
                        recipb[:, p, jj:jj + 1]))
                ev[("pscale", gp)] = S.vpad_inc()

            def s1_exp(u):
                gu = gu0 + u
                par = gu % 2
                S.wait("S", "T", ev[("scores", gu)])
                pe = ev.get(("pt", gu - 2))
                if pe is not None:
                    S.wait("S", "T", pe)
                for half in range(2):
                    last = S.op("S", lambda s, p=par, h=half: s.activation(
                        pmbuf[p][:, 2 * h:2 * h + 2, :],
                        ps_s[p][:, 512 * h:512 * (h + 1)].rearrange(
                            "p (a b) -> p a b", b=256),
                        AF.Exp, bias=cbias[:], scale=0.125),
                        inc=1 if half == 1 else 0)
                ev[("exp", gu)] = last
                rec(("ss", par), "S", last)

            def t2_pt(u):
                gu = gu0 + u
                par = gu % 2
                S.wait("T", "V", ev[("pscale", gu)])
                gwait("T", ("pt", par))
                for j in range(4):
                    for cc in range(2):
                        idx = 2 * j + cc
                        last = S.op("T", lambda t, p=par, jj=j, c=cc, i=idx: t.transpose(
                            ps_pt[p][:, 128 * i:128 * (i + 1)],
                            pmbuf[p][:, jj, 128 * c:128 * (c + 1)], ident[:]),
                            inc=1 if idx == 7 else 0)
                ev[("pt", gu)] = last

            def s2_ptevac(u):
                gu = gu0 + u
                par = gu % 2
                S.wait("S", "T", ev[("pt", gu)])
                pe = ev.get(("pv", gu - 2))
                if pe is not None:
                    S.wait("S", "T", pe)
                e = S.op("S", lambda s, p=par: s.copy(ptbuf[p][:], ps_pt[p].ap()), inc=1)
                ev[("ptevac", gu)] = e
                rec(("pt", par), "S", e)

            def t3_pv(u):
                gu = gu0 + u
                par = gu % 2
                hp = u // 2
                S.wait("T", "S", ev[("ptevac", gu)])
                gwait("T", ("ab", par))
                first = True
                for jq in range(2):
                    for cc in range(2):
                        for h_in in range(2):
                            j = 2 * h_in + jq
                            qb = unit_qb(u, j)
                            idx = 2 * j + cc
                            reg = 2 * jq + h_in
                            is_last = (jq, cc, h_in) == (1, 1, 1)
                            last = S.op("T", lambda t, p=par, c=cc, q=qb, hh=hp,
                                        i=idx, r=reg, st=first, sp=is_last: t.matmul(
                                ps_ab[p][:, 128 * r:128 * (r + 1)],
                                v_tm[:, q + c, 128 * hh:128 * (hh + 1)],
                                ptbuf[p][:, 128 * i:128 * (i + 1)],
                                start=st, stop=sp, skip_group_check=True),
                                inc=1 if is_last else 0)
                            first = False
                ev[("pv", gu)] = last

            def s3_headsevac(u):
                gu = gu0 + u
                par = gu % 2
                hp = u // 2
                qh = u % 2
                S.wait("S", "T", ev[("pv", gu)])
                for h_in in range(2):
                    for jq in range(2):
                        e = S.op("S", lambda s, p=par, hh=hp, q=qh, gg=g, hi=h_in,
                                 jj=jq: s.copy(
                            heads[64 * hi:64 * hi + 64, 4 * gg + hh,
                                  256 * q + 128 * jj:256 * q + 128 * jj + 128],
                            ps_ab[p][64 * hi:64 * hi + 64,
                                     128 * (2 * jj + hi):128 * (2 * jj + hi) + 128]),
                            inc=1 if (h_in, jq) == (1, 1) else 0)
                rec(("ab", par), "S", e)

            for u in range(UN + 2):
                if debug and ti == 0 and g == 0 and u == 1:
                    S.wait("Y", "S", ev[("exp", gu0)])
                    S.dma("Y", "c", lambda y: y.dma_start(out=dbg["pb"], in_=pmbuf[0][:]))
                    for e in "TVS":
                        S.wait(e, "c", S.cnt["c"])
                if debug and ti == 0 and g == 0 and u == 2:
                    S.wait("Y", "V", ev[("pscale", gu0)])
                    S.dma("Y", "c", lambda y: y.dma_start(out=dbg["pm"], in_=pmbuf[0][:]))
                    S.dma("Y", "c", lambda y: y.dma_start(out=dbg["sums"], in_=sumsb[:]))
                    for e in "TVS":
                        S.wait(e, "c", S.cnt["c"])
                if u < UN:
                    t1_scores(u)
                if 0 <= u - 1 < UN:
                    v3_stt(u - 1)
                v_mid(u)
                if u < UN:
                    s1_exp(u)
                if 0 <= u - 1 < UN:
                    t2_pt(u - 1)
                    s2_ptevac(u - 1)
                if 0 <= u - 2 < UN:
                    t3_pv(u - 2)
                    s3_headsevac(u - 2)
            ev[("attnT_g", ti, g)] = S.cnt["T"]

        attn_T_done = S.cnt["T"]
        heads_done_s = S.cnt["S"]
        ev[("attnT_g", ti, NG - 1)] = attn_T_done

        for hs in range(HS):
            par = hs % 2
            wsl = hs % 3
            if ti + 1 < NT:
                if hs == 36:
                    emit_ln_stats(ti + 1, 1)
                elif hs == 40:
                    emit_ln_app(ti + 1, 1)
                elif hs == 44:
                    emit_ln_stats(ti + 1, 2)
                elif hs == 48:
                    emit_ln_app(ti + 1, 2)
                elif hs == 52:
                    emit_ln_stats(ti + 1, 3)
                elif hs == 56:
                    emit_ln_stats(ti + 1, 4)
            pe = ev.get(("wgu_free", ti, hs - 3))
            if pe is None and hs < 3:
                S.wait("G", "T", ev[("slab_free", ti, 11)])
            elif pe is not None:
                S.wait("G", "T", pe)
            ld_g = S.dma("G", f"wgu{wsl}", lambda gq, h=hs, p=wsl: gq.dma_start(
                out=wg_slot[p][:], in_=wg[:, h]))
            ld_u = S.dma("G", f"wgu{wsl}", lambda gq, h=hs, p=wsl: gq.dma_start(
                out=wu_slot[p][:], in_=wu[:, h]))
            S.wait("T", f"wgu{wsl}", ld_u)
            gwait("T", ("ss", par))
            for dinc in range(DC):
                S.op("T", lambda t, k=dinc, p=par, w=wsl: t.matmul(
                    ps_s[p][:, 0:512], wg_slot[w][:, k, :], xn_fm[:, k, 128:128 + TT],
                    start=(k == 0), stop=(k == DC - 1)))
            for dinc in range(DC):
                last = S.op("T", lambda t, k=dinc, p=par, w=wsl: t.matmul(
                    ps_s[p][:, 512:1024], wu_slot[w][:, k, :], xn_fm[:, k, 128:128 + TT],
                    start=(k == 0), stop=(k == DC - 1)),
                    inc=1 if dinc == DC - 1 else 0)
            ev[("wgu_free", ti, hs)] = last
            S.wait("S", "T", last)
            sil = S.op("S", lambda s, p=par: s.activation(
                silu_t[p][:], ps_s[p][:, 0:512], AF.Silu), inc=1)
            rec(("ss", par), "S", sil)
            S.wait("V", "S", sil)
            hm = S.op("V", lambda v, h=hs, p=par: v.tensor_mul(
                hbuf[:, h, :], silu_t[p][:], ps_s[p][:, 512:1024]), inc=1)
            rec(("ss", par), "V", hm)
        ffn_h_done_v = S.vpad_inc()
        gate_T_done = S.cnt["T"]
        if debug and ti == 0:
            S.wait("Y", "S", heads_done_s)
            S.dma("Y", "c", lambda y: y.dma_start(out=dbg["heads"], in_=heads[:]))
            S.wait("Y", "V", ffn_h_done_v)
            S.dma("Y", "c", lambda y: y.dma_start(out=dbg["hbuf"], in_=hbuf[:]))
            for e in "TVS":
                S.wait(e, "c", S.cnt["c"])

        for ds in range(DC):
            par = ds % 2
            osl = (ds + 1) % 2
            pe = ev.get(("wod_free", ti, ds - 2))
            if pe is not None:
                S.wait("G", "T", pe)
            elif ds == 0:
                S.wait("G", "T", ev[("slab_free", ti, 11)])
            elif ds == 1:
                S.wait("G", "T", ev[("wgu_free", ti, HS - 1)])
            ld_d = S.dma("G", f"wd{osl}", lambda gq, d=ds, p=osl: gq.dma_start(
                out=wd_slot[p][:], in_=wd[:, d]))
            if ds < 2:
                S.wait("G", "T", ev[("wgu_free", ti, HS - 1 if ds == 0 else HS - 3)])
            ld_o = S.dma("G", f"wo{par}", lambda gq, d=ds, p=par: gq.dma_start(
                out=wo_slot[p][:], in_=wo[:, d]))
            pst_x = ev.get(("store", ti, ds - 2))
            if pst_x is None and ds < 2:
                pst_x = ev.get(("store", ti - 1, DC - 2 + ds))
            if pst_x is not None:
                S.wait("Y", f"st{par}", pst_x)
            ld_x = S.dma("Y", f"xf{par}", lambda y, d=ds, t=ti, p=par: y.dma_start(
                out=xfmb[p][:], in_=x_fm[128 * d:128 * (d + 1), TT * t:TT * (t + 1)]))
            S.wait("T", f"wd{osl}", ld_d)
            S.wait("T", f"wo{par}", ld_o)
            if ds == 0:
                S.wait("T", "V", ffn_h_done_v)
                S.wait("T", "S", heads_done_s)
            gwait("T", ("ab", par))
            for hdc in range(DC):
                S.op("T", lambda t, k=hdc, p=par: t.matmul(
                    ps_ab[p][:], wo_slot[p][:, k, :], heads[:, k, :],
                    start=(k == 0), stop=False))
            for hc in range(HS):
                last = S.op("T", lambda t, k=hc, p=par, o=osl: t.matmul(
                    ps_ab[p][:], wd_slot[o][:, k, :], hbuf[:, k, :],
                    start=False, stop=(k == HS - 1)),
                    inc=1 if hc == HS - 1 else 0)
            ev[("wod_free", ti, ds)] = last
            S.wait("V", "T", last)
            S.wait("V", f"xf{par}", ld_x)
            pst = ev.get(("store", ti, ds - 2))
            if pst is None and ds < 2:
                pst = ev.get(("store", ti - 1, DC - 2 + ds))
            if pst is not None:
                S.wait("V", f"st{par}", pst)
            S.op("V", lambda v, p=par: v.tensor_add(
                xfmb[p][:], ps_ab[p][:], xfmb[p][:]))
            oe = S.vpad_inc()
            ev[("outevac", ti, ds)] = oe
            rec(("ab", par), "V", oe)
            S.wait("Y", "V", oe)
            ev[("store", ti, ds)] = S.dma("Y", f"st{par}", lambda y, d=ds, t=ti, p=par: y.dma_start(
                out=o_fm[128 * d:128 * (d + 1), TT * t:TT * (t + 1)], in_=xfmb[p][:]))
        if ti + 1 < NT:
            emit_ln_tr(ti + 1, 1)
            emit_ln_app(ti + 1, 3)
            emit_ln_tr(ti + 1, 2)
            emit_ln_app(ti + 1, 4)
            emit_ln_tr(ti + 1, 3)
            emit_ln_tr(ti + 1, 4)
            ev[("ln_done", ti + 1)] = S.cnt["S"]
        arena_T_free = S.cnt["T"]

    for sname in ["c", "xr0", "xr1", "xf0", "xf1", "st0", "st1"]:
        if S.cnt.get(sname):
            S.raw_wait("Y", sname, S.cnt[sname])
    for sname in ["qk0", "qk1", "wgu0", "wgu1", "wgu2", "wo0", "wo1", "wd0", "wd1"]:
        if S.cnt.get(sname):
            S.raw_wait("G", sname, S.cnt[sname])

    with nc.Block() as block:
        @block.tensor
        def _(t):
            S.emit(t, "T", sems)

        @block.vector
        def _(v):
            S.emit(v, "V", sems)

        @block.scalar
        def _(s):
            S.emit(s, "S", sems)

        @block.gpsimd
        def _(gq):
            S.emit(gq, "G", sems)

        @block.sync
        def _(y):
            S.emit(y, "Y", sems)

    es.close()
    return nc


def _prep_weights(Wq, Wk, Wv, Wo, Wg, Wu, Wd):
    def qkv_layout(W):
        return np.ascontiguousarray(W.reshape(NG, GD, DC, 128).transpose(0, 3, 2, 1))

    return (qkv_layout(Wq), qkv_layout(Wk), qkv_layout(Wv),
            np.ascontiguousarray(Wo.reshape(DC, 128, DC, 128).transpose(3, 0, 2, 1)),
            np.ascontiguousarray(Wg.reshape(HS, 128, DC, 128).transpose(3, 0, 2, 1)),
            np.ascontiguousarray(Wu.reshape(HS, 128, DC, 128).transpose(3, 0, 2, 1)),
            np.ascontiguousarray(Wd.reshape(DC, 128, HS, 128).transpose(3, 0, 2, 1)))


def _masks():
    r = np.arange(128)[:, None]
    c = np.arange(256)[None, :]
    band = (r <= c) & (c <= r + WIN)
    m_std = band.astype(np.float32).astype(BF)
    m_first = (band & (c >= 128)).astype(np.float32).astype(BF)
    return m_std, m_first


def kernel(L0_attn_x, L0_attn_Wq, L0_attn_Wk, L0_attn_Wv, L0_attn_Wo,
           L0_ffn_w_gate, L0_ffn_w_up, L0_ffn_w_down):
    x = np.asarray(L0_attn_x)
    wq, wk, wv, wo, wg, wu, wd = _prep_weights(
        np.asarray(L0_attn_Wq), np.asarray(L0_attn_Wk), np.asarray(L0_attn_Wv),
        np.asarray(L0_attn_Wo), np.asarray(L0_ffn_w_gate), np.asarray(L0_ffn_w_up),
        np.asarray(L0_ffn_w_down))
    m_std, m_first = _masks()
    ident = np.eye(128, dtype=BF)

    nc = build_nc()
    in_maps = []
    for c in range(8):
        b = c // 4
        s = (c % 4) * TPC
        if s == 0:
            halo = np.zeros((128, D), dtype=np.float32)
            mf = m_first
        else:
            halo = x[b, s - 128:s, :]
            mf = m_std
        x_tok = np.ascontiguousarray(np.concatenate([halo, x[b, s:s + TPC, :]], axis=0))
        x_fm = np.ascontiguousarray(x[b, s:s + TPC, :].T)
        in_maps.append({
            "x_tok": x_tok, "x_fm": x_fm,
            "wq": wq, "wk": wk, "wv": wv, "wo": wo,
            "wg": wg, "wu": wu, "wd": wd,
            "mask_std": m_std, "mask_first": mf, "ident": ident,
        })

    res = run_bass_kernel_spmd(nc, in_maps, core_ids=list(range(8)))
    out = np.empty((2, 8192, D), dtype=np.float32)
    for c in range(8):
        b = c // 4
        s = (c % 4) * TPC
        out[b, s:s + TPC, :] = np.asarray(res.results[c]["o_fm"]).T
    return out
